# version 57 (final)
# baseline (speedup 1.0000x reference)
import math
import numpy as np
import ml_dtypes
from contextlib import ExitStack

import concourse.bass as bass
import concourse.mybir as mybir
from concourse.bass_utils import run_bass_kernel_spmd

F32 = mybir.dt.float32
BF16 = mybir.dt.bfloat16
AF = mybir.ActivationFunctionType
ALU = mybir.AluOpType
AX = mybir.AxisListType

N = 2048
D = 1024
NCH = 16
PH = 16
PD = 64
PM = 64
FH = 8
FD = 64
FI = 512
WIN = 512
HID = 2730
HID2 = 5460
LN8 = math.log(8.0)

ENGS = ("sync", "vector", "scalar", "tensor", "gpsimd")


def COPY(E, out, in_):
    return E.copy(out, in_) if hasattr(E, "copy") and not hasattr(E, "tensor_copy") \
        else E.tensor_copy(out, in_)

DBG = set()
SA_HALVES = 2
SA_P2 = 16
SA_STEP = 99


def host_tables():
    half = np.arange(0, FD, 2, dtype=np.float32)
    inv_freq = (1.0 / (10000.0 ** (half / FD))).astype(np.float32)
    t = np.arange(N, dtype=np.float32)
    fr = t[:, None] * inv_freq[None, :]
    freqs = np.concatenate([fr, fr], 1).astype(np.float32)
    base = ((half + 0.4 * FD) / (1.4 * FD)).astype(np.float32)
    power = ((t - N // 2) / float(WIN // 2)).astype(np.float32)
    scale = (base[None, :] ** power[:, None]).astype(np.float32)
    scale = np.concatenate([scale, scale], 1).astype(np.float32)
    cos = np.cos(freqs).astype(np.float32)
    sin = np.sin(freqs).astype(np.float32)
    QK = np.float32(8.0)
    tcq = (cos * scale) * QK
    tsq = (sin * scale) * QK
    tck = cos / scale
    tsk = sin / scale
    for tt in (tsq, tsk):
        tt[:, :32] *= np.float32(-1.0)

    def chunkmajor(a):
        return np.ascontiguousarray(
            a.reshape(NCH, 128, FD).transpose(1, 0, 2).reshape(128, NCH * FD)
        ).astype(np.float32)

    return {"tcq": chunkmajor(tcq), "tsq": chunkmajor(tsq),
            "tck": chunkmajor(tck), "tsk": chunkmajor(tsk)}


def host_consts():
    c = {}
    c["ident"] = np.eye(128, dtype=ml_dtypes.bfloat16)
    c["ident_f"] = np.eye(128, dtype=np.float32)
    tri_u = np.triu(np.ones((128, 128), dtype=np.float32))
    c["m_up_f32"] = tri_u
    c["m_low_bf"] = np.tril(np.ones((128, 128))).astype(ml_dtypes.bfloat16)
    c["m_up_bf"] = tri_u.astype(ml_dtypes.bfloat16)
    ones128 = np.ones((128, 128), np.float32)
    tri_l = np.tril(np.ones((128, 128), np.float32))
    c["m_cfull"] = np.concatenate(
        [tri_l, ones128, ones128, ones128, tri_u], axis=1).astype(ml_dtypes.bfloat16)
    c["m_ctail"] = np.concatenate(
        [ones128, ones128, ones128, ones128, tri_u], axis=1).astype(ml_dtypes.bfloat16)
    c.update(host_tables())
    return c


class Buf:
    __slots__ = ("w", "r", "name")

    def __init__(self, name=""):
        self.name = name
        self.w = {}
        self.r = {}


class Sched:
    DMA_POOL = 8

    def __init__(self, nc, ctx):
        self.nc = nc
        self.q = {e: [] for e in ENGS}
        self.sems = {}
        self.cnt = {}
        self.lastwait = {e: {} for e in ENGS}
        self.dma_k = {}
        for e in ENGS:
            self.mk_sem("c_" + e, ctx)
        for qn in ("dS", "dW", "dT", "dG"):
            self.dma_k[qn] = 0
            for i in range(self.DMA_POOL):
                self.mk_sem(f"{qn}{i}", ctx)

    def mk_sem(self, name, ctx):
        self.sems[name] = ctx.enter_context(self.nc.semaphore(name))
        self.cnt[name] = 0

    def dma_flush(self, eng, qn):
        for i in range(self.DMA_POOL):
            self._wait(eng, f"{qn}{i}", self.cnt[f"{qn}{i}"])

    def barrier(self, engines=ENGS):
        for e in engines:
            for s, v in self.cnt.items():
                self._wait(e, s, v)

    def _wait(self, eng, sem, val):
        if val <= 0:
            return
        if self.lastwait[eng].get(sem, 0) >= val:
            return
        self.lastwait[eng][sem] = val
        h = self.sems[sem]
        self.q[eng].append(lambda E, h=h, v=val: E.wait_ge(h, v))

    def _wait_map(self, eng, m, war=False):
        for sem, (val, peng) in m.items():
            self._wait(eng, sem, val)

    def do(self, eng, fn, reads=(), writes=(), dma=None):
        for b in reads:
            self._wait_map(eng, b.w)
        for b in writes:
            self._wait_map(eng, b.r)
            self._wait_map(eng, b.w)
        if dma is not None:
            k = self.dma_k[dma]
            self.dma_k[dma] = k + 1
            sem = f"{dma}{k % self.DMA_POOL}"
            self._wait(eng, sem, self.cnt[sem])
            self.cnt[sem] += 16
            val = self.cnt[sem]
            h = self.sems[sem]

            def run(E, fn=fn, h=h):
                inst = fn(E)
                inst.then_inc(h, 16)
            self.q[eng].append(run)
            tok = (sem, val, "dma")
        else:
            sem = "c_" + eng
            self.cnt[sem] += 1
            val = self.cnt[sem]
            h = self.sems[sem]

            def run(E, fn=fn, h=h):
                inst = fn(E)
                inst.then_inc(h, 1)
            self.q[eng].append(run)
            tok = (sem, val, eng)
        for b in writes:
            b.w[tok[0]] = (tok[1], tok[2])
            b.r = {}
        for b in reads:
            b.r[tok[0]] = (tok[1], tok[2])
        return tok


class Defer:
    def __init__(self):
        self.items = []

    def do(self, *a, **k):
        self.items.append((a, k))


def build():
    nc = bass.Bass("TRN2", target_bir_lowering=False)

    x_h = nc.dram_tensor("x", [N, D], F32, kind="ExternalInput").ap()
    h_in = {}
    for nm, shape in [
        ("wqT", [D, D]), ("wkT", [D, D]), ("wvT", [D, D]), ("woT", [D, D]),
        ("om_top", [128, PM]), ("om_bot", [128, PM]), ("bo", [1, D]),
        ("wqkv0", [D, 3 * FI]), ("wqkv1", [D, 3 * FI]),
        ("wof0", [FI, D]), ("wof1", [FI, D]),
        ("w1s0", [44 * 128, D]), ("w1s1", [44 * 128, D]),
        ("b1_0", [1, HID2]), ("b1_1", [1, HID2]),
        ("b1c0", [128, 44]), ("b1c1", [128, 44]),
        ("w2_0", [HID, D]), ("w2_1", [HID, D]),
        ("b2_0", [1, D]), ("b2_1", [1, D]),
        ("lng0", [128, 8]), ("lng1", [128, 8]),
        ("lnb0", [128, 8]), ("lnb1", [128, 8]),
        ("qs0", [1, FD]), ("qs1", [1, FD]),
        ("ks0", [1, FD]), ("ks1", [1, FD]),
        ("ident", [128, 128]), ("ident_f", [128, 128]), ("m_up_f32", [128, 128]),
        ("m_low_bf", [128, 128]), ("m_up_bf", [128, 128]),
        ("m_cfull", [128, 640]), ("m_ctail", [128, 640]),
        ("tcq", [128, NCH * FD]), ("tsq", [128, NCH * FD]),
        ("tck", [128, NCH * FD]), ("tsk", [128, NCH * FD]),
    ]:
        dt = BF16 if nm in ("ident", "m_low_bf", "m_up_bf",
                            "m_cfull", "m_ctail") else F32
        h_in[nm] = nc.dram_tensor(nm, shape, dt, kind="ExternalInput").ap()

    out_h = nc.dram_tensor("out", [N, D], F32, kind="ExternalOutput").ap()
    xn_dram = nc.dram_tensor(
        "xn_dram", [N, D], F32,
        kind="ExternalOutput" if "xn" in DBG else "Internal").ap()
    xA_dram = nc.dram_tensor(
        "xA_dram", [N, D], F32,
        kind="ExternalOutput" if "xA" in DBG else "Internal").ap()
    xf_dram = nc.dram_tensor(
        "xf_dram", [N, D], F32,
        kind="ExternalOutput" if "xf" in DBG else "Internal").ap()
    hb_dram = nc.dram_tensor("hb_dram", [8 * 128, D], BF16, kind="Internal").ap()
    xf2_dram = nc.dram_tensor(
        "xf2_dram", [N, D], F32,
        kind="ExternalOutput" if "xfm" in DBG else "Internal").ap()

    dbg_out = {}
    for nm in DBG:
        if nm.startswith("sb:"):
            pass

    ctx = ExitStack()
    with ctx:
        def sb(name, shape, dt):
            return ctx.enter_context(nc.sbuf_tensor(name, shape, dt))

        def psb(name, shape, dt):
            return ctx.enter_context(nc.psum_tensor(name, shape, dt))

        S = Sched(nc, ctx)

        srcT = sb("srcT", [128, 8 * N], BF16)
        ident = sb("idents", [128, 128], BF16)
        ident_f = sb("idents_f", [128, 128], F32)
        m_up_f32 = sb("m_up_f32s", [128, 128], F32)
        m_low_bf = sb("m_low_bfs", [128, 128], BF16)
        m_up_bf = sb("m_up_bfs", [128, 128], BF16)
        m_cfull = sb("m_cfulls", [128, 640], BF16)
        m_ctail = sb("m_ctails", [128, 640], BF16)
        ones_row = sb("ones_row", [1, 128], BF16)
        CPOOL_VALS = {"eps_ln": 1e-5, "neg_ln8": -LN8, "zero": 0.0}
        cpool = sb("cpool", [128, len(CPOOL_VALS)], F32)
        cpool_ix = {k: i for i, k in enumerate(CPOOL_VALS)}

        def cp(name, p0=0, p1=128):
            i = cpool_ix[name]
            return cpool[p0:p1, i:i + 1]

        PB = [psb(f"PB{i}", [128, 512], F32) for i in range(8)]
        PBbf = [p.ap().bitcast(BF16) for p in PB]

        bufs = {}

        def B(name):
            if name not in bufs:
                bufs[name] = Buf(name)
            return bufs[name]

        def prep():
            for nm, dst in [("ident", ident), ("ident_f", ident_f),
                            ("m_up_f32", m_up_f32),
                            ("m_low_bf", m_low_bf), ("m_up_bf", m_up_bf),
                            ("m_cfull", m_cfull), ("m_ctail", m_ctail)]:
                S.do("sync", lambda E, d=dst, s=h_in[nm]: E.dma_start(d[:], s),
                     writes=[B("c_" + nm)], dma="dS")
            S.do("vector", lambda E: E.memset(ones_row[:], 1.0),
                 writes=[B("ones_row")])
            for k, v in CPOOL_VALS.items():
                S.do("vector", lambda E, k=k, v=v: E.memset(cp(k), float(v)),
                     writes=[B("cpool")])

        def stage0(sc):
            ssb = lambda nm, shp, dt: sc.enter_context(nc.sbuf_tensor(nm, shp, dt))
            xin = [ssb(f"s0_xin{i}", [128, D], F32) for i in range(2)]
            xnf = [ssb(f"s0_xnf{i}", [128, D], F32) for i in range(2)]
            xnb = [ssb(f"s0_xnb{i}", [128, D], BF16) for i in range(2)]
            st = [ssb(f"s0_st{i}", [128, 8], F32) for i in range(2)]

            def s0_front(c):
                sl = c % 2
                bx = B(f"s0_xin{sl}")
                S.do("sync", lambda E, sl=sl, c=c: E.dma_start(
                    xin[sl][:], x_h[c * 128:(c + 1) * 128, :]), writes=[bx], dma="dS")
                bst = B(f"s0_st{sl}")
                bxn = B(f"s0_xnf{sl}")
                S.do("scalar", lambda E, sl=sl: E.activation(
                    xnf[sl][:], xin[sl][:], AF.Square, accum_out=st[sl][:, 0:1]),
                    reads=[bx], writes=[bxn, bst])
                S.do("vector", lambda E, sl=sl: E.reduce_sum(
                    st[sl][:, 1:2], xin[sl][:], axis=AX.X), reads=[bx], writes=[bst])
                S.do("vector", lambda E, sl=sl: E.tensor_scalar(
                    st[sl][:, 2:3], st[sl][:, 1:2], 1.0 / D, None, ALU.mult),
                    reads=[bst], writes=[bst])
                S.do("vector", lambda E, sl=sl: E.tensor_tensor(
                    st[sl][:, 3:4], st[sl][:, 2:3], st[sl][:, 2:3], ALU.mult),
                    reads=[bst], writes=[bst])
                S.do("vector", lambda E, sl=sl: E.scalar_tensor_tensor(
                    st[sl][:, 4:5], st[sl][:, 0:1], 1.0 / D, st[sl][:, 3:4],
                    ALU.mult, ALU.subtract), reads=[bst], writes=[bst])
                S.do("scalar", lambda E, sl=sl: E.activation(
                    st[sl][:, 5:6], st[sl][:, 4:5], AF.Ln, bias=cp("eps_ln")),
                    reads=[bst, B("cpool")], writes=[bst])
                S.do("scalar", lambda E, sl=sl: E.activation(
                    st[sl][:, 6:7], st[sl][:, 5:6], AF.Exp, scale=-0.5),
                    reads=[bst], writes=[bst])
                S.do("vector", lambda E, sl=sl: E.tensor_tensor(
                    st[sl][:, 7:8], st[sl][:, 2:3], st[sl][:, 6:7], ALU.mult),
                    reads=[bst], writes=[bst])
                bxn = B(f"s0_xnf{sl}")
                S.do("vector", lambda E, sl=sl: E.tensor_scalar(
                    xnf[sl][:], xin[sl][:], st[sl][:, 6:7], st[sl][:, 7:8],
                    ALU.mult, ALU.subtract), reads=[bx, bst], writes=[bxn])
                S.do("gpsimd", lambda E, sl=sl, c=c: E.dma_start(
                    xn_dram[c * 128:(c + 1) * 128, :], xnf[sl][:]),
                    reads=[bxn], dma="dG")
                bxb = B(f"s0_xnb{sl}")
                S.do("gpsimd", lambda E, sl=sl: E.dma_start(xnb[sl][:], xnf[sl][:]),
                     reads=[bxn], writes=[bxb], dma="dW")

            def s0_back(c):
                sl = c % 2
                bxb = B(f"s0_xnb{sl}")
                srcT3 = srcT[:].rearrange("p (t n) -> p t n", t=8)
                for grp in range(4):
                    pb = 6 + (grp % 2)
                    bpb = B(f"pb{pb}")
                    for j in range(2):
                        ic = grp * 2 + j
                        S.do("tensor", lambda E, sl=sl, ic=ic, j=j, pb=pb: E.transpose(
                            PBbf[pb][:, 768 + j * 128: 768 + (j + 1) * 128],
                            xnb[sl][:, ic * 128:(ic + 1) * 128],
                            ident[:]), reads=[bxb, B("c_ident")], writes=[bpb])
                    eng = "vector" if grp % 2 == 0 else "scalar"
                    S.do(eng, lambda E, c=c, grp=grp, pb=pb: COPY(
                        E, srcT3[:, grp * 2: grp * 2 + 2, c * 128:(c + 1) * 128],
                        PBbf[pb][:, 768:1024].rearrange("p (t n) -> p t n", t=2)),
                        reads=[bpb],
                        writes=[B("srcT"), B(f"srcT_h{c // 8}")])

            return s0_front, s0_back, xin

        def stageA(sc, s0_front, s0_back, s0_xin):
            ssb = lambda nm, shp, dt: sc.enter_context(nc.sbuf_tensor(nm, shp, dt))
            wsl = [ssb(f"sa_wsl{i}", [128, 8 * 128], BF16) for i in range(2)]
            wv_bf = ssb("sa_wv", [128, 8 * 1024], BF16)
            woT_bf = ssb("sa_wo", [128, 8 * 1024], BF16)
            om_t = ssb("sa_omt", [128, PM], BF16)
            om_b = ssb("sa_omb", [128, PM], BF16)
            neg_t = ssb("sa_negt", [128, PM], BF16)
            neg_b = ssb("sa_negb", [128, PM], BF16)
            bo_row = ssb("sa_bo", [1, D], BF16)
            qT_sb = ssb("sa_qT", [128, 8 * 1024], BF16)
            kT_sb = ssb("sa_kT", [128, 8 * 1024], BF16)
            vaug = ssb("sa_vaug", [128, NCH * 1040], BF16)
            phikT = [ssb(f"sa_phikT{i}", [128, PH * 128], BF16) for i in range(2)]
            phiqT = [ssb(f"sa_phiqT{i}", [128, PH * 128], BF16) for i in range(2)]
            phiktok = [ssb(f"sa_phiktok{i}", [128, 1024], BF16) for i in range(2)]
            ksqh = [ssb(f"sa_ksqh{i}", [128, 1024], BF16) for i in range(2)]
            qsqh = [ssb(f"sa_qsqh{i}", [128, 1024], BF16) for i in range(2)]
            ATm0 = ssb("sa_ATm", [128, 16 * 128], BF16)
            ATm = [ATm0, ATm0]
            Sbf = [ssb(f"sa_Sbf{i}", [128, PH * 65], BF16) for i in range(2)]
            dent = ssb("sa_dent", [128, 16], F32)
            drec = ssb("sa_drec", [128, 16], F32)
            pair = [ssb(f"sa_pair{i}", [128, 8 * 128], BF16) for i in range(2)]
            innT = [ssb(f"sa_innT{i}", [128, 8 * 128], BF16) for i in range(2)]
            xa_in = s0_xin
            xa_out = xa_in

            kT3 = kT_sb[:].rearrange("p (t c) -> p t c", t=8)
            qT3 = qT_sb[:].rearrange("p (t c) -> p t c", t=8)

            S.do("gpsimd", lambda E: E.dma_start(
                wv_bf[:].rearrange("p (t c) -> p t c", t=8),
                h_in["wvT"].rearrange("(t p) c -> p t c", p=128)),
                writes=[B("sa_wv")], dma="dW")
            S.do("gpsimd", lambda E: E.dma_start(
                woT_bf[:].rearrange("p (t c) -> p t c", t=8),
                h_in["woT"].rearrange("(t p) c -> p t c", p=128)),
                writes=[B("sa_wo")], dma="dW")
            S.do("gpsimd", lambda E: E.dma_start(om_t[:], h_in["om_top"]),
                 writes=[B("sa_om")], dma="dW")
            S.do("gpsimd", lambda E: E.dma_start(om_b[:], h_in["om_bot"]),
                 writes=[B("sa_om")], dma="dW")
            S.do("gpsimd", lambda E: E.dma_start(bo_row[:], h_in["bo"]),
                 writes=[B("sa_bo")], dma="dW")
            S.do("vector", lambda E: E.memset(neg_t[:], 0.0), writes=[B("sa_neg")])
            S.do("vector", lambda E: E.memset(neg_t[0:64, :], -1.0), writes=[B("sa_neg")])
            S.do("vector", lambda E: E.memset(neg_b[:], 0.0), writes=[B("sa_neg")])
            S.do("vector", lambda E: E.memset(neg_b[64:128, :], -1.0), writes=[B("sa_neg")])
            for i in range(2):
                S.do("vector", lambda E, i=i: E.memset(phikT[i][:], 0.0),
                     writes=[B(f"sa_phikT{i}")])
                S.do("vector", lambda E, i=i: E.memset(phiqT[i][:], 0.0),
                     writes=[B(f"sa_phiqT{i}")])
                S.do("vector", lambda E, i=i: E.memset(Sbf[i][:], 0.0),
                     writes=[B(f"sa_Sbf{i}")])
            vaug4 = vaug[:].rearrange("p (c h e) -> p c h e", c=NCH, h=PH)
            S.do("vector", lambda E: E.memset(vaug4[:, :, :, 64:65], 1.0),
                 writes=[B("sa_vaug_ones")])

            def load_wslice(mat, ic, sl, Q):
                Q.do("gpsimd", lambda E: E.dma_start(
                    wsl[sl][:].rearrange("p (t c) -> p t c", t=8),
                    h_in[mat].rearrange("(t p) c -> p t c", p=128)[:, :, ic * 128:(ic + 1) * 128]),
                    reads=[], writes=[B(f"sa_wsl{sl}")], dma="dW")

            def p1_half(half, Q):
                t0 = half * 1024
                bsrc = B(f"srcT_h{half}")
                for mat, dst, dstb in (("wqT", qT_sb, "sa_qT"), ("wkT", kT_sb, "sa_kT")):
                    for ic in range(8):
                        sl = ic % 2
                        load_wslice(mat, ic, sl, Q)
                        pA, pB = (0, 1) if ic % 2 == 0 else (2, 3)

                        def mmgrp(E, sl=sl, pA=pA, pB=pB, t0=t0):
                            last = None
                            for kc in range(8):
                                lhs = wsl[sl][:, kc * 128:(kc + 1) * 128]
                                E.matmul(PB[pA][:], lhs,
                                         srcT[:, kc * N + t0: kc * N + t0 + 512],
                                         start=(kc == 0), stop=(kc == 7))
                                last = E.matmul(PB[pB][:], lhs,
                                                srcT[:, kc * N + t0 + 512: kc * N + t0 + 1024],
                                                start=(kc == 0), stop=(kc == 7))
                            return last
                        Q.do("tensor", mmgrp,
                             reads=[B(f"sa_wsl{sl}"), bsrc],
                             writes=[B(f"pb{pA}"), B(f"pb{pB}")])
                        Q.do("vector", lambda E, ic=ic, pA=pA, dst=dst: E.tensor_copy(
                            dst[:, ic * 1024: ic * 1024 + 512], PB[pA][:]),
                            reads=[B(f"pb{pA}")], writes=[B(dstb)])
                        Q.do("scalar", lambda E, ic=ic, pB=pB, dst=dst: E.copy(
                            dst[:, ic * 1024 + 512: ic * 1024 + 1024], PB[pB][:]),
                            reads=[B(f"pb{pB}")], writes=[B(dstb)])
            def vproj(cg):
                def vgrp(E, cg=cg):
                    last = None
                    for kc in range(8):
                        lhs = srcT[:, kc * N + cg * 128: kc * N + (cg + 1) * 128]
                        E.matmul(PB[4][:], lhs, wv_bf[:, kc * 1024: kc * 1024 + 512],
                                 start=(kc == 0), stop=(kc == 7))
                        last = E.matmul(PB[5][:], lhs,
                                        wv_bf[:, kc * 1024 + 512: kc * 1024 + 1024],
                                        start=(kc == 0), stop=(kc == 7))
                    return last
                S.do("tensor", vgrp, reads=[B("sa_wv"), B(f"srcT_h{cg // 8}")],
                     writes=[B("pb4"), B("pb5")])
                for bank, h0, eng in ((4, 0, "vector"), (5, 8, "scalar")):
                    S.do(eng, lambda E, cg=cg, bank=bank, h0=h0: COPY(
                        E, vaug4[:, cg, h0:h0 + 8, 0:64],
                        PB[bank][:].rearrange("p (h e) -> p h e", h=8)),
                        reads=[B(f"pb{bank}")], writes=[B("sa_vaug")])

            def sb_off(h):
                return (0 if h < 8 else 260) + ((h // 2) % 4) * 65

            def p2_front(c, Q):
                tc = (c % 8) * 128
                sl = c % 2
                P = lambda nm: B(f"{nm}{sl}")
                Q.do("scalar", lambda E, sl=sl, tc=tc: E.activation(
                    ksqh[sl][:], kT3[:, :, tc:tc + 128], AF.Square,
                    scale=0.7071067811865476),
                    reads=[B("sa_kT")], writes=[P("sa_ksqh")])
                Q.do("scalar", lambda E, sl=sl, tc=tc: E.activation(
                    qsqh[sl][:], qT3[:, :, tc:tc + 128], AF.Square,
                    scale=0.7071067811865476),
                    reads=[B("sa_qT")], writes=[P("sa_qsqh")])
                for (nm, srcb, sq, dst, dstb) in (
                        ("fkT", "sa_kT", ksqh[sl], phikT[sl], f"sa_phikT{sl}"),
                        ("fqT", "sa_qT", qsqh[sl], phiqT[sl], f"sa_phiqT{sl}")):
                    srcT3 = kT3 if srcb == "sa_kT" else qT3
                    sqb = f"sa_ksqh{sl}" if srcb == "sa_kT" else f"sa_qsqh{sl}"
                    for bi, bank in enumerate((0, 1)):
                        def fgrp(E, bi=bi, bank=bank, srcT3=srcT3, sq=sq, tc=tc):
                            E.matmul(PB[bank][0:64, :], om_t[:],
                                     srcT3[:, bi * 4:(bi + 1) * 4, tc:tc + 128],
                                     start=True, stop=False)
                            E.matmul(PB[bank][0:64, :], neg_t[:],
                                     sq[:, bi * 512:(bi + 1) * 512],
                                     start=False, stop=True)
                            E.matmul(PB[bank][64:128, :], om_b[:],
                                     srcT3[:, bi * 4:(bi + 1) * 4, tc:tc + 128],
                                     start=True, stop=False)
                            return E.matmul(PB[bank][64:128, :], neg_b[:],
                                            sq[:, bi * 512:(bi + 1) * 512],
                                            start=False, stop=True)
                        Q.do("tensor", fgrp,
                             reads=[B(srcb), B(sqb), B("sa_om"), B("sa_neg")],
                             writes=[B(f"pb{bank}")])
                        phz = dst[:].rearrange("p (t q c) -> p t q c", t=8, q=2)
                        Q.do("scalar", lambda E, bi=bi, bank=bank, phz=phz: E.activation(
                            phz[0:64, 4 * bi:4 * bi + 4, 0, :],
                            PB[bank][0:64, :].rearrange("p (t c) -> p t c", t=4),
                            AF.Exp, bias=cp("neg_ln8", 0, 64)),
                            reads=[B(f"pb{bank}"), B("cpool")], writes=[B(dstb)])
                        Q.do("scalar", lambda E, bi=bi, bank=bank, phz=phz: E.activation(
                            phz[64:128, 4 * bi:4 * bi + 4, 1, :],
                            PB[bank][64:128, :].rearrange("p (t c) -> p t c", t=4),
                            AF.Exp, bias=cp("neg_ln8", 64, 128)),
                            reads=[B(f"pb{bank}"), B("cpool")], writes=[B(dstb)])
                for bank, h0 in ((0, 0), (1, 8)):
                    def tgrp(E, bank=bank, h0=h0, tc=tc, sl=sl):
                        last = None
                        for i in range(8):
                            h = h0 + i
                            t = h // 2
                            omx = om_t if h % 2 == 0 else om_b
                            ngx = neg_t if h % 2 == 0 else neg_b
                            E.matmul(PB[bank][:, i * 64:(i + 1) * 64],
                                     kT3[:, t, tc:tc + 128], omx[:],
                                     start=True, stop=False)
                            last = E.matmul(PB[bank][:, i * 64:(i + 1) * 64],
                                            ksqh[sl][:, t * 128:(t + 1) * 128], ngx[:],
                                            start=False, stop=True)
                        return last
                    Q.do("tensor", tgrp,
                         reads=[B("sa_kT"), P("sa_ksqh"), B("sa_om"), B("sa_neg")],
                         writes=[B(f"pb{bank}")])
                    Q.do("scalar", lambda E, bank=bank, h0=h0, sl=sl: E.activation(
                        phiktok[sl][:, h0 * 64:(h0 + 8) * 64], PB[bank][:], AF.Exp,
                        bias=cp("neg_ln8")),
                        reads=[B(f"pb{bank}"), B("cpool")], writes=[P("sa_phiktok")])

            def p2_back(c, dq):
                sl = c % 2
                P = lambda nm: B(f"{nm}{sl}")

                def drain(n=3):
                    for _ in range(n):
                        if dq:
                            a, k = dq.pop(0)
                            S.do(*a, **k)

                if c > 0:
                    sbz = Sbf[sl][:].rearrange("p (t q e) -> p t q e", t=8, q=2)
                    for bank, t0 in ((6, 0), (7, 4)):
                        S.do("scalar", lambda E, bank=bank, t0=t0, sbz=sbz: E.copy(
                            sbz[0:64, t0:t0 + 4, 0, :],
                            PB[bank][0:64, 0:260].rearrange("p (s e) -> p s e", s=4)),
                            reads=[B(f"pb{bank}")], writes=[P("sa_Sbf")])
                        S.do("scalar", lambda E, bank=bank, t0=t0, sbz=sbz: E.copy(
                            sbz[64:128, t0:t0 + 4, 1, :],
                            PB[bank][64:128, 0:260].rearrange("p (s e) -> p s e", s=4)),
                            reads=[B(f"pb{bank}")], writes=[P("sa_Sbf")])
                drain()
                for g in range(4):
                    bank = 2 + g % 2

                    def agrp(E, g=g, sl=sl, bank=bank):
                        last = None
                        for i in range(4):
                            h = 4 * g + i
                            last = E.matmul(PB[bank][:, i * 128:(i + 1) * 128],
                                            phikT[sl][:, h * 128:(h + 1) * 128],
                                            phiqT[sl][:, h * 128:(h + 1) * 128],
                                            start=True, stop=True)
                        return last
                    S.do("tensor", agrp, reads=[P("sa_phikT"), P("sa_phiqT")],
                         writes=[B(f"pb{bank}")])
                    S.do("vector", lambda E, g=g, sl=sl, bank=bank: E.tensor_tensor(
                        ATm[sl][:, g * 512:(g + 1) * 512].rearrange(
                            "p (i c) -> p i c", i=4),
                        PB[bank][:].rearrange("p (i c) -> p i c", i=4),
                        m_up_f32[:].unsqueeze(1).broadcast_to([128, 4, 128]),
                        ALU.mult),
                        reads=[B(f"pb{bank}"), B("c_m_up_f32")], writes=[B("sa_ATm")])
                    drain()
                for g in range(4):
                    bank = 2 + g % 2

                    def ogrp(E, g=g, c=c, sl=sl, bank=bank):
                        last = None
                        for i in range(4):
                            h = 4 * g + i
                            last = E.matmul(PB[bank][:, i * 65: i * 65 + 65],
                                            ATm[sl][:, h * 128:(h + 1) * 128],
                                            vaug[:, c * 1040 + h * 65: c * 1040 + h * 65 + 65],
                                            start=True, stop=(c == 0))
                            if c > 0:
                                last = E.matmul(PB[bank][:, i * 65: i * 65 + 65],
                                                phiqT[sl][:, h * 128:(h + 1) * 128],
                                                Sbf[sl][:, h * 65: h * 65 + 65],
                                                start=False, stop=True)
                        return last
                    S.do("tensor", ogrp,
                         reads=[B("sa_ATm"), P("sa_phiqT"), B("sa_vaug"),
                                B("sa_vaug_ones"), P("sa_Sbf")],
                         writes=[B(f"pb{bank}")])
                    pbg3 = PB[bank][:, 0:260].rearrange("p (i e) -> p i e", i=4)
                    S.do("vector", lambda E, g=g, pbg3=pbg3: E.tensor_scalar(
                        dent[:, g * 4:(g + 1) * 4].rearrange("p (i e) -> p i e", e=1),
                        pbg3[:, :, 64:65], 1e-6, None, ALU.add),
                        reads=[B(f"pb{bank}")], writes=[B("sa_dent")])
                    S.do("vector", lambda E, g=g: E.reciprocal(
                        drec[:, g * 4:(g + 1) * 4], dent[:, g * 4:(g + 1) * 4]),
                        reads=[B("sa_dent")], writes=[B("sa_drec")])
                    S.do("vector", lambda E, g=g, pbg3=pbg3, sl=sl: E.scalar_tensor_tensor(
                        pair[sl][:, g * 256:(g + 1) * 256].rearrange(
                            "p (i c) -> p i c", i=4),
                        pbg3[:, :, 0:64], 1.0,
                        drec[:, g * 4:(g + 1) * 4].unsqueeze(2).broadcast_to(
                            [128, 4, 64]),
                        ALU.mult, ALU.mult),
                        reads=[B(f"pb{bank}"), B("sa_drec")], writes=[P("sa_pair")])
                    drain()
                for bank, h0 in ((6, 0), (7, 8)):
                    def sgrp(E, bank=bank, h0=h0, c=c, sl=sl):
                        last = None
                        for i in range(8):
                            h = h0 + i
                            par = 64 * (h % 2)
                            slot = (h // 2) % 4
                            last = E.matmul(
                                PB[bank][par:par + 64, slot * 65: slot * 65 + 65],
                                phiktok[sl][:, h * 64:(h + 1) * 64],
                                vaug[:, c * 1040 + h * 65: c * 1040 + h * 65 + 65],
                                start=False, stop=True,
                                skip_group_check=True)
                        return last
                    S.do("tensor", sgrp,
                         reads=[P("sa_phiktok"), B("sa_vaug"), B("sa_vaug_ones")],
                         writes=[B(f"pb{bank}")])
                    drain()
                for grp, pb, eng in ((0, 4, "vector"), (1, 5, "scalar")):
                    for j in range(4):
                        p = grp * 4 + j
                        S.do("tensor", lambda E, p=p, j=j, pb=pb, sl=sl: E.transpose(
                            PBbf[pb][:, j * 128:(j + 1) * 128],
                            pair[sl][:, p * 128:(p + 1) * 128], ident[:]),
                            reads=[P("sa_pair"), B("c_ident")], writes=[B(f"pb{pb}")])
                    S.do(eng, lambda E, grp=grp, pb=pb, sl=sl: COPY(
                        E, innT[sl][:, grp * 512:(grp + 1) * 512], PBbf[pb][:, 0:512]),
                        reads=[B(f"pb{pb}")], writes=[P("sa_innT")])
                    drain(4)
                sl2 = c % 2
                S.do("sync", lambda E, c=c, sl=sl2: E.dma_start(
                    xa_in[sl][:], x_h[c * 128:(c + 1) * 128, :]),
                    writes=[B(f"s0_xin{sl}")], dma="dS")

                def wogrp(E, sl=sl):
                    for ic in range(8):
                        lhs = innT[sl][:, ic * 128:(ic + 1) * 128]
                        E.matmul(PB[4][:], lhs, woT_bf[:, ic * 1024: ic * 1024 + 512],
                                 start=(ic == 0), stop=False)
                        E.matmul(PB[5][:], lhs, woT_bf[:, ic * 1024 + 512: ic * 1024 + 1024],
                                 start=(ic == 0), stop=False)
                    E.matmul(PB[4][:], ones_row[:], bo_row[:, 0:512],
                             start=False, stop=True)
                    return E.matmul(PB[5][:], ones_row[:], bo_row[:, 512:1024],
                                    start=False, stop=True)
                S.do("tensor", wogrp,
                     reads=[P("sa_innT"), B("sa_wo"), B("sa_bo"), B("ones_row")],
                     writes=[B("pb4"), B("pb5")])
                S.do("vector", lambda E, sl=sl: E.tensor_tensor(
                    xa_out[sl][:, 0:512], PB[4][:], xa_in[sl][:, 0:512], ALU.add),
                    reads=[B("pb4")], writes=[P("s0_xin")])
                S.do("vector", lambda E, sl=sl: E.tensor_tensor(
                    xa_out[sl][:, 512:1024], PB[5][:], xa_in[sl][:, 512:1024], ALU.add),
                    reads=[B("pb5")], writes=[P("s0_xin")])
                S.do("gpsimd", lambda E, c=c, sl=sl: E.dma_start(
                    xA_dram[c * 128:(c + 1) * 128, :], xa_out[sl][:]),
                    reads=[P("s0_xin")], dma="dG")
                while dq:
                    a, k = dq.pop(0)
                    S.do(*a, **k)
            for half in range(min(2, SA_HALVES)):
                if half == 0:
                    s0_front(0)
                    for c in range(8):
                        s0_front(c + 1)
                        s0_back(c)
                        vproj(c)
                    dp1 = Defer()
                    p1_half(0, dp1)
                    it = dp1.items
                    per = (len(it) + 7) // 8
                    k = 0
                    for c in range(8, 16):
                        for _ in range(per):
                            if k < len(it):
                                a, kw = it[k]
                                S.do(*a, **kw)
                                k += 1
                        if c + 1 < 16:
                            s0_front(c + 1)
                        s0_back(c)
                        vproj(c)
                    while k < len(it):
                        a, kw = it[k]
                        S.do(*a, **kw)
                        k += 1
                    S.do("vector", lambda E: E.memset(PB[6][:], 0.0),
                         writes=[B("pb6")])
                    S.do("vector", lambda E: E.memset(PB[7][:], 0.0),
                         writes=[B("pb7")])
                else:
                    p1_half(half, S)
                for cc in range(8):
                    c = half * 8 + cc
                    if c >= SA_P2:
                        continue
                    if cc == 0:
                        d = Defer()
                        p2_front(c, d)
                        for a, k in d.items:
                            S.do(*a, **k)
                    dn = Defer()
                    if cc + 1 < 8 and c + 1 < SA_P2:
                        p2_front(c + 1, dn)
                    p2_back(c, dn.items)


        def stageFa(sc, l):
            ssb = lambda nm, shp, dt: sc.enter_context(
                nc.sbuf_tensor(f"l{l}" + nm, shp, dt))
            base_dram = xn_dram if l == 0 else xf2_dram
            base_buf = "base_done" if l == 0 else "xf2_dram"

            tmast = {nm: ssb("fa_tm_" + nm, [128, NCH * FD], F32)
                     for nm in ("tcq", "tsq", "tck", "tsk")}
            for nm in ("tcq", "tsq", "tck", "tsk"):
                S.do("sync", lambda E, nm=nm: E.dma_start(
                    tmast[nm][:], h_in[nm]), writes=[B("c_" + nm)], dma="dS")
            wqkv_bf = ssb("fa_wqkv", [128, 8 * 1536], BF16)
            wof_bf = ssb("fa_wof", [128, 4 * 1024], BF16)
            tq1 = ssb("fa_tq1", [128, NCH * FD], F32)
            tq2 = ssb("fa_tq2", [128, NCH * FD], F32)
            tk1 = ssb("fa_tk1", [128, NCH * FD], F32)
            tk2 = ssb("fa_tk2", [128, NCH * FD], F32)
            qsr = ssb("fa_qsr", [128, FD], F32)
            qsp = ssb("fa_qsp", [128, FD], F32)
            ksr = ssb("fa_ksr", [128, FD], F32)
            ksp = ssb("fa_ksp", [128, FD], F32)
            srow = ssb("fa_srow", [1, 2 * FD], F32)
            ones_f32 = ssb("fa_ones", [1, 128], F32)
            hsh0 = ssb("fa_hsh", [128, 4 * 128], BF16)
            hsh = [hsh0, hsh0]
            vaugF = ssb("fa_vaug", [128, NCH * 520], BF16)
            kTf = ssb("fa_kTf", [128, 4 * N], BF16)
            qTz = [ssb(f"fa_qTz{i}", [128, FH * 128], BF16) for i in range(2)]

            k_f32 = [ssb(f"fa_kf32{i}", [128, FI], F32) for i in range(2)]
            q_f32 = [ssb(f"fa_qf32{i}", [128, FI], F32) for i in range(2)]
            ssq = [ssb(f"fa_ssq{i}", [128, 2 * FH], F32) for i in range(2)]
            inv = [ssb(f"fa_inv{i}", [128, 2 * FH], F32) for i in range(2)]
            a_q = [ssb(f"fa_aq{i}", [128, FI], F32) for i in range(2)]
            b_q = [ssb(f"fa_bq{i}", [128, FI], F32) for i in range(2)]
            a_k = [ssb(f"fa_ak{i}", [128, FI], F32) for i in range(2)]
            b_k = [ssb(f"fa_bk{i}", [128, FI], F32) for i in range(2)]
            u_q = q_f32
            u_k = k_f32
            qrot = [ssb(f"fa_qrot{i}", [128, FI], BF16) for i in range(2)]
            krot = [ssb(f"fa_krot{i}", [128, FI], BF16) for i in range(2)]
            pbuf = [ssb(f"fa_pbuf{i}", [128, 640], BF16) for i in range(4)]
            pT = [ssb(f"fa_pT{i}", [128, 5 * 128], BF16) for i in range(2)]
            nmx = ssb("fa_nmx", [128, 2 * FH], F32)
            dent = ssb("fa_dent", [128, FH], F32)
            drec = ssb("fa_drec", [128, FH], F32)
            o_pair = [ssb(f"fa_opair{i}", [128, FI], BF16) for i in range(2)]
            oT = [ssb(f"fa_oT{i}", [128, 4 * 128], BF16) for i in range(2)]
            xr_in = [ssb(f"fa_xrin{i}", [128, D], F32) for i in range(2)]
            xf_out = xr_in

            vaugF4 = vaugF[:].rearrange("p (c h e) -> p c h e", c=NCH, h=FH)
            tq1c = tq1[:].rearrange("p (c e) -> p c e", c=NCH)
            tq2c = tq2[:].rearrange("p (c e) -> p c e", c=NCH)
            tk1c = tk1[:].rearrange("p (c e) -> p c e", c=NCH)
            tk2c = tk2[:].rearrange("p (c e) -> p c e", c=NCH)

            S.do("gpsimd", lambda E: E.dma_start(
                wqkv_bf[:].rearrange("p (t c) -> p t c", t=8),
                h_in[f"wqkv{l}"].rearrange("(t p) c -> p t c", p=128)),
                writes=[B("fa_wqkv")], dma="dW")
            S.do("gpsimd", lambda E: E.dma_start(
                wof_bf[:].rearrange("p (t c) -> p t c", t=4),
                h_in[f"wof{l}"].rearrange("(t p) c -> p t c", p=128)),
                writes=[B("fa_wof")], dma="dW")
            S.do("sync", lambda E: E.dma_start(srow[:, 0:FD], h_in[f"qs{l}"]),
                 writes=[B("fa_srow")], dma="dS")
            S.do("sync", lambda E: E.dma_start(srow[:, FD:2 * FD], h_in[f"ks{l}"]),
                 writes=[B("fa_srow")], dma="dS")
            S.do("vector", lambda E: E.memset(ones_f32[:], 1.0), writes=[B("fa_ones")])
            S.do("vector", lambda E: E.memset(qTz[0][:], 0.0), writes=[B("fa_qTz0")])
            S.do("vector", lambda E: E.memset(qTz[1][:], 0.0), writes=[B("fa_qTz1")])
            S.do("vector", lambda E: E.memset(vaugF4[:, :, :, 64:65], 1.0),
                 writes=[B("fa_vaug_ones")])
            S.do("tensor", lambda E: E.matmul(
                PB[0][:, 0:2 * FD], ones_f32[:], srow[:], start=True, stop=True),
                reads=[B("fa_srow"), B("fa_ones")], writes=[B("pb0")])
            S.do("vector", lambda E: E.tensor_copy(qsr[:], PB[0][:, 0:FD]),
                 reads=[B("pb0")], writes=[B("fa_qsr")])
            S.do("vector", lambda E: E.tensor_copy(ksr[:], PB[0][:, FD:2 * FD]),
                 reads=[B("pb0")], writes=[B("fa_ksr")])
            for rep, per, brep, bper in ((qsr, qsp, "fa_qsr", "fa_qsp"),
                                         (ksr, ksp, "fa_ksr", "fa_ksp")):
                S.do("vector", lambda E, rep=rep, per=per: E.tensor_copy(
                    per[:, 0:32], rep[:, 32:64]), reads=[B(brep)], writes=[B(bper)])
                S.do("vector", lambda E, rep=rep, per=per: E.tensor_copy(
                    per[:, 32:64], rep[:, 0:32]), reads=[B(brep)], writes=[B(bper)])
            for dst, bd, mast, rep, brep in (
                    (tq1, "fa_tq1", "tcq", qsr, "fa_qsr"),
                    (tq2, "fa_tq2", "tsq", qsp, "fa_qsp"),
                    (tk1, "fa_tk1", "tck", ksr, "fa_ksr"),
                    (tk2, "fa_tk2", "tsk", ksp, "fa_ksp")):
                S.do("vector", lambda E, dst=dst, mast=mast, rep=rep: E.tensor_tensor(
                    dst[:].rearrange("p (c e) -> p c e", c=NCH),
                    tmast[mast][:].rearrange("p (c e) -> p c e", c=NCH),
                    rep[:].unsqueeze(1).broadcast_to([128, NCH, FD]),
                    ALU.mult),
                    reads=[B("c_" + mast), B(brep)], writes=[B(bd)])

            pfh = [ssb(f"fa_pfh{i}", [128, D], F32) for i in range(2)]
            pfst = [ssb(f"fa_pfst{i}", [128, 8], F32) for i in range(2)]
            pfb = [ssb(f"fa_pfb{i}", [128, D], BF16) for i in range(2)]

            def pf_front(cg, Q):
                sl = cg % 2
                bh = B(f"fa_pfh{sl}")
                bb = B(f"fa_pfb{sl}")
                bst = B(f"fa_pfst{sl}")
                Q.do("sync", lambda E, cg=cg, sl=sl: E.dma_start(
                    pfh[sl][:, 0:512], xf_dram[cg * 128:(cg + 1) * 128, 0:512]),
                    reads=[B("xf_dram")], writes=[bh], dma="dS")
                if cg > 0:
                    Q.do("sync", lambda E, cg=cg, sl=sl: E.dma_start(
                        pfh[sl][:, 512:1024],
                        xf_dram[cg * 128 - 1:(cg + 1) * 128 - 1, 512:1024]),
                        reads=[B("xf_dram")], writes=[bh], dma="dS")
                else:
                    Q.do("vector", lambda E, sl=sl: E.memset(
                        pfh[sl][0:1, 512:1024], 0.0), writes=[bh])
                    Q.do("sync", lambda E, sl=sl: E.dma_start(
                        pfh[sl][1:128, 512:1024], xf_dram[0:127, 512:1024]),
                        reads=[B("xf_dram")], writes=[bh], dma="dS")
                Q.do("scalar", lambda E, sl=sl: E.activation(
                    pfb[sl][:], pfh[sl][:], AF.Square, accum_out=pfst[sl][:, 0:1]),
                    reads=[bh], writes=[bb, bst])
                Q.do("vector", lambda E, sl=sl: E.reduce_sum(
                    pfst[sl][:, 1:2], pfh[sl][:], axis=AX.X), reads=[bh], writes=[bst])
                Q.do("vector", lambda E, sl=sl: E.tensor_scalar(
                    pfst[sl][:, 2:3], pfst[sl][:, 1:2], -1.0 / D, None, ALU.mult),
                    reads=[bst], writes=[bst])
                Q.do("vector", lambda E, sl=sl: E.tensor_tensor(
                    pfst[sl][:, 3:4], pfst[sl][:, 2:3], pfst[sl][:, 2:3], ALU.mult),
                    reads=[bst], writes=[bst])
                Q.do("vector", lambda E, sl=sl: E.scalar_tensor_tensor(
                    pfst[sl][:, 4:5], pfst[sl][:, 0:1], 1.0 / D, pfst[sl][:, 3:4],
                    ALU.mult, ALU.subtract), reads=[bst], writes=[bst])
                Q.do("scalar", lambda E, sl=sl: E.activation(
                    pfst[sl][:, 5:6], pfst[sl][:, 4:5], AF.Ln, bias=cp("eps_ln")),
                    reads=[bst, B("cpool")], writes=[bst])
                Q.do("scalar", lambda E, sl=sl: E.activation(
                    pfst[sl][:, 6:7], pfst[sl][:, 5:6], AF.Exp, scale=-0.5),
                    reads=[bst], writes=[bst])
                Q.do("vector", lambda E, sl=sl: E.tensor_tensor(
                    pfst[sl][:, 7:8], pfst[sl][:, 2:3], pfst[sl][:, 6:7], ALU.mult),
                    reads=[bst], writes=[bst])
                Q.do("scalar", lambda E, sl=sl: E.activation(
                    pfb[sl][:], pfh[sl][:], AF.Identity, scale=pfst[sl][:, 6:7],
                    bias=pfst[sl][:, 7:8]), reads=[bh, bst], writes=[bb])
                Q.do("gpsimd", lambda E, cg=cg, sl=sl: E.dma_start(
                    hb_dram[cg * 128:(cg + 1) * 128, :], pfb[sl][:]),
                    reads=[bb], writes=[B("hb_dram")], dma="dG")

            def fa_front(c, Qpre, Q):
                sl = c % 2
                P = lambda nm: B(f"{nm}{sl}")
                Qpre.do("sync", lambda E, c=c, sl=sl: E.dma_start(
                    xr_in[sl][:], base_dram[c * 128:(c + 1) * 128, :]),
                    reads=[B(base_buf)], writes=[P("fa_xrin")], dma="dS")
                if c == 0:
                    for t in range(4, 8):
                        Qpre.do("vector", lambda E, t=t, sl=sl: E.memset(
                            hsh[sl][:, (t - 4) * 128:(t - 4) * 128 + 1], 0.0),
                            writes=[P("fa_hsh")])
                        Qpre.do("vector", lambda E, t=t, sl=sl: E.tensor_copy(
                            hsh[sl][:, (t - 4) * 128 + 1:(t - 3) * 128],
                            srcT[:, t * N: t * N + 127]),
                            reads=[B("srcT")], writes=[P("fa_hsh")])

                def qkvgrp(E, c=c, sl=sl):
                    last = None
                    for kc in range(8):
                        if kc < 4:
                            lhs = srcT[:, kc * N + c * 128: kc * N + (c + 1) * 128]
                        elif c == 0:
                            lhs = hsh[sl][:, (kc - 4) * 128:(kc - 3) * 128]
                        else:
                            lhs = srcT[:, kc * N + c * 128 - 1: kc * N + (c + 1) * 128 - 1]
                        E.matmul(PB[0][:], lhs, wqkv_bf[:, kc * 1536: kc * 1536 + 512],
                                 start=(kc == 0), stop=(kc == 7))
                        E.matmul(PB[1][:], lhs, wqkv_bf[:, kc * 1536 + 512: kc * 1536 + 1024],
                                 start=(kc == 0), stop=(kc == 7))
                        last = E.matmul(PB[2][:], lhs,
                                        wqkv_bf[:, kc * 1536 + 1024: kc * 1536 + 1536],
                                        start=(kc == 0), stop=(kc == 7))
                    return last
                Qpre.do("tensor", qkvgrp, reads=[B("srcT"), P("fa_hsh"), B("fa_wqkv")],
                     writes=[B("pb0"), B("pb1"), B("pb2")])
                Qpre.do("scalar", lambda E, c=c: E.copy(
                    vaugF4[:, c, :, 0:64],
                    PB[2][:].rearrange("p (h e) -> p h e", h=FH)),
                    reads=[B("pb2")], writes=[B(f"fa_vaug_{c}")])
                Qpre.do("scalar", lambda E, sl=sl: E.copy(k_f32[sl][:], PB[1][:]),
                     reads=[B("pb1")], writes=[P("fa_kf32")])
                Qpre.do("scalar", lambda E, sl=sl: E.copy(
                    q_f32[sl][:], PB[0][:]), reads=[B("pb0")], writes=[P("fa_qf32")])

                Q.do("scalar", lambda E, sl=sl: E.activation(
                    a_q[sl][:], q_f32[sl][:], AF.Square),
                    reads=[P("fa_qf32")], writes=[P("fa_aq")])
                Q.do("vector", lambda E, sl=sl: E.reduce_sum(
                    ssq[sl][:, 0:FH].rearrange("p (h e) -> p h e", e=1),
                    a_q[sl][:].rearrange("p (h e) -> p h e", h=FH), axis=AX.X),
                    reads=[P("fa_aq")], writes=[P("fa_ssq")])
                Q.do("gpsimd", lambda E, sl=sl: E.tensor_tensor(
                    a_k[sl][:], k_f32[sl][:], k_f32[sl][:], ALU.mult),
                    reads=[P("fa_kf32")], writes=[P("fa_ak")])
                Q.do("vector", lambda E, sl=sl: E.reduce_sum(
                    ssq[sl][:, FH:2 * FH].rearrange("p (h e) -> p h e", e=1),
                    a_k[sl][:].rearrange("p (h e) -> p h e", h=FH), axis=AX.X),
                    reads=[P("fa_ak")], writes=[P("fa_ssq")])
                Q.do("vector", lambda E, sl=sl: E.tensor_scalar_max(
                    ssq[sl][:], ssq[sl][:], 1e-24),
                    reads=[P("fa_ssq")], writes=[P("fa_ssq")])
                Q.do("scalar", lambda E, sl=sl: E.activation(
                    inv[sl][:], ssq[sl][:], AF.Ln),
                    reads=[P("fa_ssq")], writes=[P("fa_inv")])
                Q.do("scalar", lambda E, sl=sl: E.activation(
                    inv[sl][:], inv[sl][:], AF.Exp, scale=-0.5),
                    reads=[P("fa_inv")], writes=[P("fa_inv")])
                Q.do("gpsimd", lambda E, sl=sl: E.tensor_tensor(
                    u_q[sl][:].rearrange("p (h e) -> p h e", h=FH),
                    q_f32[sl][:].rearrange("p (h e) -> p h e", h=FH),
                    inv[sl][:, 0:FH].unsqueeze(2).broadcast_to([128, FH, FD]),
                    ALU.mult), reads=[P("fa_qf32"), P("fa_inv")],
                    writes=[P("fa_qf32")])
                Q.do("gpsimd", lambda E, c=c, sl=sl: E.tensor_tensor(
                    a_q[sl][:].rearrange("p (h e) -> p h e", h=FH),
                    u_q[sl][:].rearrange("p (h e) -> p h e", h=FH),
                    tq1c[:, c, :].unsqueeze(1).broadcast_to([128, FH, FD]),
                    ALU.mult), reads=[P("fa_qf32"), B("fa_tq1")], writes=[P("fa_aq")])
                uq3 = u_q[sl][:].rearrange("p (h e) -> p h e", h=FH)
                bq3 = b_q[sl][:].rearrange("p (h e) -> p h e", h=FH)
                Q.do("gpsimd", lambda E, c=c, uq3=uq3, bq3=bq3: E.tensor_tensor(
                    bq3[:, :, 0:32], uq3[:, :, 32:64],
                    tq2c[:, c, 0:32].unsqueeze(1).broadcast_to([128, FH, 32]),
                    ALU.mult), reads=[P("fa_qf32"), B("fa_tq2")], writes=[P("fa_bq")])
                Q.do("gpsimd", lambda E, c=c, uq3=uq3, bq3=bq3: E.tensor_tensor(
                    bq3[:, :, 32:64], uq3[:, :, 0:32],
                    tq2c[:, c, 32:64].unsqueeze(1).broadcast_to([128, FH, 32]),
                    ALU.mult), reads=[P("fa_qf32"), B("fa_tq2")], writes=[P("fa_bq")])
                Q.do("gpsimd", lambda E, sl=sl: E.tensor_add(
                    qrot[sl][:], a_q[sl][:], b_q[sl][:]),
                    reads=[P("fa_aq"), P("fa_bq")], writes=[P("fa_qrot")])
                Q.do("gpsimd", lambda E, sl=sl: E.tensor_tensor(
                    u_k[sl][:].rearrange("p (h e) -> p h e", h=FH),
                    k_f32[sl][:].rearrange("p (h e) -> p h e", h=FH),
                    inv[sl][:, FH:2 * FH].unsqueeze(2).broadcast_to([128, FH, FD]),
                    ALU.mult), reads=[P("fa_kf32"), P("fa_inv")],
                    writes=[P("fa_kf32")])
                Q.do("gpsimd", lambda E, c=c, sl=sl: E.tensor_tensor(
                    a_k[sl][:].rearrange("p (h e) -> p h e", h=FH),
                    u_k[sl][:].rearrange("p (h e) -> p h e", h=FH),
                    tk1c[:, c, :].unsqueeze(1).broadcast_to([128, FH, FD]),
                    ALU.mult), reads=[P("fa_kf32"), B("fa_tk1")], writes=[P("fa_ak")])
                uk3 = u_k[sl][:].rearrange("p (h e) -> p h e", h=FH)
                bk3 = b_k[sl][:].rearrange("p (h e) -> p h e", h=FH)
                Q.do("gpsimd", lambda E, c=c, uk3=uk3, bk3=bk3: E.tensor_tensor(
                    bk3[:, :, 0:32], uk3[:, :, 32:64],
                    tk2c[:, c, 0:32].unsqueeze(1).broadcast_to([128, FH, 32]),
                    ALU.mult), reads=[P("fa_kf32"), B("fa_tk2")], writes=[P("fa_bk")])
                Q.do("gpsimd", lambda E, c=c, uk3=uk3, bk3=bk3: E.tensor_tensor(
                    bk3[:, :, 32:64], uk3[:, :, 0:32],
                    tk2c[:, c, 32:64].unsqueeze(1).broadcast_to([128, FH, 32]),
                    ALU.mult), reads=[P("fa_kf32"), B("fa_tk2")], writes=[P("fa_bk")])
                Q.do("gpsimd", lambda E, sl=sl: E.tensor_add(
                    krot[sl][:], a_k[sl][:], b_k[sl][:]),
                    reads=[P("fa_ak"), P("fa_bk")], writes=[P("fa_krot")])

            def fa_mid(c, Q):
                sl = c % 2
                P = lambda nm: B(f"{nm}{sl}")
                for p in range(4):
                    Q.do("tensor", lambda E, p=p, sl=sl: E.transpose(
                        PBbf[2][:, p * 128:(p + 1) * 128],
                        qrot[sl][:, p * 128:(p + 1) * 128], ident[:]),
                        reads=[P("fa_qrot"), B("c_ident")], writes=[B("pb2")])
                Q.do("scalar", lambda E, sl=sl: E.copy(
                    qTz[sl][0:64, :].rearrange("p (t c) -> p t c", t=8)[:, 0::2, :],
                    PBbf[2][0:64, 0:512].rearrange("p (t c) -> p t c", t=4)),
                    reads=[B("pb2")], writes=[P("fa_qTz")])
                Q.do("scalar", lambda E, sl=sl: E.copy(
                    qTz[sl][64:128, :].rearrange("p (t c) -> p t c", t=8)[:, 1::2, :],
                    PBbf[2][64:128, 0:512].rearrange("p (t c) -> p t c", t=4)),
                    reads=[B("pb2")], writes=[P("fa_qTz")])
                for p in range(4):
                    Q.do("tensor", lambda E, p=p, sl=sl: E.transpose(
                        PBbf[3][:, p * 128:(p + 1) * 128],
                        krot[sl][:, p * 128:(p + 1) * 128], ident[:]),
                        reads=[P("fa_krot"), B("c_ident")], writes=[B("pb3")])
                Q.do("scalar", lambda E, c=c: E.copy(
                    kTf[:].rearrange("p (t c) -> p t c", c=N)[:, :, c * 128:(c + 1) * 128],
                    PBbf[3][:, 0:512].rearrange("p (t c) -> p t c", t=4)),
                    reads=[B("pb3")], writes=[B(f"fa_kTf_{c}")])

            def fa_back(c, dq):
                sl = c % 2
                P = lambda nm: B(f"{nm}{sl}")
                kc0 = max(0, c - 4)
                nk = (c - kc0 + 1) * 128
                n1 = min(nk, 512)
                ntile = nk // 128

                def head_front(h):
                    hp = h % 4
                    sA, sB = (4, 5) if h % 2 == 0 else (0, 1)
                    tp = h // 2

                    def scgrp(E, h=h, tp=tp, kc0=kc0, nk=nk, n1=n1,
                              sA=sA, sB=sB, sl=sl):
                        last = E.matmul(PB[sA][:, 0:n1], qTz[sl][:, h * 128:(h + 1) * 128],
                                        kTf[:, tp * N + kc0 * 128: tp * N + kc0 * 128 + n1],
                                        start=True, stop=True)
                        if nk > 512:
                            last = E.matmul(PB[sB][:, 0:128], qTz[sl][:, h * 128:(h + 1) * 128],
                                            kTf[:, tp * N + kc0 * 128 + 512: tp * N + kc0 * 128 + nk],
                                            start=True, stop=True)
                        return last
                    S.do("tensor", scgrp,
                         reads=[P("fa_qTz")] + [B(f"fa_kTf_{j}")
                                                for j in range(kc0, c + 1)],
                         writes=[B(f"pb{sA}")] + ([B(f"pb{sB}")] if nk > 512 else []))
                    S.do("vector", lambda E, h=h, n1=n1, sA=sA: E.reduce_max(
                        nmx[:, h:h + 1], PB[sA][:, 0:n1], axis=AX.X, negate=True),
                        reads=[B(f"pb{sA}")], writes=[B("fa_nmx")])
                    if nk > 512:
                        S.do("vector", lambda E, h=h, sB=sB: E.reduce_max(
                            nmx[:, FH + h:FH + h + 1], PB[sB][:, 0:128], axis=AX.X,
                            negate=True),
                            reads=[B(f"pb{sB}")], writes=[B("fa_nmx")])
                        S.do("vector", lambda E, h=h: E.tensor_tensor(
                            nmx[:, h:h + 1], nmx[:, h:h + 1],
                            nmx[:, FH + h:FH + h + 1], ALU.min),
                            reads=[B("fa_nmx")], writes=[B("fa_nmx")])
                    S.do("scalar", lambda E, h=h, n1=n1, sA=sA, hp=hp: E.activation(
                        pbuf[hp][:, 0:n1], PB[sA][:, 0:n1], AF.Exp,
                        bias=nmx[:, h:h + 1]),
                        reads=[B(f"pb{sA}"), B("fa_nmx")], writes=[B(f"fa_pbuf{hp}")])
                    if nk > 512:
                        S.do("scalar", lambda E, h=h, sB=sB, hp=hp: E.activation(
                            pbuf[hp][:, 512:640], PB[sB][:, 0:128], AF.Exp,
                            bias=nmx[:, h:h + 1]),
                            reads=[B(f"pb{sB}"), B("fa_nmx")], writes=[B(f"fa_pbuf{hp}")])

                def head_back(h):
                    hp = h % 4
                    pb = 2 + (h % 2)
                    for j in range(ntile):
                        S.do("tensor", lambda E, j=j, pb=pb, hp=hp: E.transpose(
                            PBbf[pb][:, j * 128:(j + 1) * 128],
                            pbuf[hp][:, j * 128:(j + 1) * 128], ident[:]),
                            reads=[B(f"fa_pbuf{hp}"), B("c_ident")], writes=[B(f"pb{pb}")])
                    msk = m_cfull if c >= 4 else m_ctail
                    moff = 0 if c >= 4 else 640 - ntile * 128
                    S.do("vector", lambda E, pb=pb, h=h, msk=msk, moff=moff,
                         ntile=ntile: E.tensor_tensor(
                            pT[h % 2][:, 0:ntile * 128], PBbf[pb][:, 0:ntile * 128],
                            msk[:, moff:moff + ntile * 128], ALU.mult),
                            reads=[B(f"pb{pb}"), B("c_m_cfull"), B("c_m_ctail")],
                            writes=[B(f"fa_pT{h % 2}")])
                    avb = 6 + h // 4

                    def avgrp(E, h=h, kc0=kc0, ntile=ntile, avb=avb, hp=hp):
                        last = None
                        for j in range(ntile):
                            last = E.matmul(
                                PB[avb][:, (h % 4) * 65:(h % 4) * 65 + 65],
                                pT[h % 2][:, j * 128:(j + 1) * 128],
                                vaugF[:, (kc0 + j) * 520 + h * 65:(kc0 + j) * 520 + h * 65 + 65],
                                start=(j == 0), stop=(j == ntile - 1))
                        return last
                    S.do("tensor", avgrp,
                         reads=[B(f"fa_pT{h % 2}"), B("fa_vaug_ones")]
                               + [B(f"fa_vaug_{j}") for j in range(kc0, c + 1)],
                         writes=[B(f"pb{avb}")])

                def drain(n):
                    for _ in range(n):
                        if dq:
                            a, k = dq.pop(0)
                            S.do(*a, **k)
                for step in range(FH + 3):
                    if step < FH:
                        head_front(step)
                    if step >= 3:
                        head_back(step - 3)
                    if step == 3:
                        drain(4)
                    elif step > 3:
                        drain((len(dq) + FH + 1 - step) // max(1, FH + 2 - step))
                drain(len(dq))

                for g, avb in ((0, 6), (1, 7)):
                    pbg3 = PB[avb][:, 0:260].rearrange("p (i e) -> p i e", i=4)
                    S.do("vector", lambda E, g=g, pbg3=pbg3: E.tensor_scalar(
                        dent[:, g * 4:(g + 1) * 4].rearrange("p (i e) -> p i e", e=1),
                        pbg3[:, :, 64:65], 1e-30, None, ALU.add),
                        reads=[B(f"pb{avb}")], writes=[B("fa_dent")])
                    S.do("vector", lambda E, g=g: E.reciprocal(
                        drec[:, g * 4:(g + 1) * 4], dent[:, g * 4:(g + 1) * 4]),
                        reads=[B("fa_dent")], writes=[B("fa_drec")])
                    S.do("vector", lambda E, g=g, pbg3=pbg3, sl=sl: E.scalar_tensor_tensor(
                        o_pair[sl][:, g * 256:(g + 1) * 256].rearrange(
                            "p (i e) -> p i e", i=4),
                        pbg3[:, :, 0:64], 1.0,
                        drec[:, g * 4:(g + 1) * 4].unsqueeze(2).broadcast_to(
                            [128, 4, 64]),
                        ALU.mult, ALU.mult),
                        reads=[B(f"pb{avb}"), B("fa_drec")], writes=[P("fa_opair")])
                for p in range(4):
                    S.do("tensor", lambda E, p=p, sl=sl: E.transpose(
                        PBbf[2][:, p * 128:(p + 1) * 128],
                        o_pair[sl][:, p * 128:(p + 1) * 128], ident[:]),
                        reads=[P("fa_opair"), B("c_ident")], writes=[B("pb2")])
                S.do("scalar", lambda E, sl=sl: E.copy(
                    oT[sl][:], PBbf[2][:, 0:512]),
                    reads=[B("pb2")], writes=[P("fa_oT")])

                def wogrp(E, sl=sl):
                    last = None
                    for ic in range(4):
                        lhs = oT[sl][:, ic * 128:(ic + 1) * 128]
                        E.matmul(PB[6][:], lhs, wof_bf[:, ic * 1024: ic * 1024 + 512],
                                 start=(ic == 0), stop=(ic == 3))
                        last = E.matmul(PB[7][:], lhs,
                                        wof_bf[:, ic * 1024 + 512: ic * 1024 + 1024],
                                        start=(ic == 0), stop=(ic == 3))
                    return last
                S.do("tensor", wogrp, reads=[P("fa_oT"), B("fa_wof")],
                     writes=[B("pb6"), B("pb7")])
                S.do("vector", lambda E, sl=sl: E.tensor_tensor(
                    xf_out[sl][:, 0:512], PB[6][:], xr_in[sl][:, 0:512], ALU.add),
                    reads=[B("pb6")], writes=[P("fa_xrin")])
                S.do("vector", lambda E, sl=sl: E.tensor_tensor(
                    xf_out[sl][:, 512:1024], PB[7][:], xr_in[sl][:, 512:1024], ALU.add),
                    reads=[B("pb7")], writes=[P("fa_xrin")])
                S.do("gpsimd", lambda E, c=c, sl=sl: E.dma_start(
                    xf_dram[c * 128:(c + 1) * 128, :], xf_out[sl][:]),
                    reads=[P("fa_xrin")], writes=[B("xf_dram")], dma="dG")

            d0p, d0 = Defer(), Defer()
            fa_front(0, d0p, d0)
            fa_mid(0, d0)
            for a, k in d0p.items + d0.items:
                S.do(*a, **k)
            for c in range(NCH):
                dpre, dq, dmid = Defer(), Defer(), Defer()
                if c + 1 < NCH:
                    fa_front(c + 1, dpre, dq)
                    fa_mid(c + 1, dmid)
                pre = dpre.items
                if pre:
                    a, k = pre[0]
                    S.do(*a, **k)
                    pre = pre[1:]
                dpf = Defer()
                if c >= 8:
                    pf_front(c - 8, dpf)
                fa_back(c, pre + dq.items + dpf.items)
                for a, k in dmid.items:
                    S.do(*a, **k)

        def stageFm(sc, l):
            ssb = lambda nm, shp, dt: sc.enter_context(
                nc.sbuf_tensor(f"l{l}" + nm, shp, dt))
            final = (l == 1)
            NHC = 22
            h2nT = [ssb(f"fm_h2nT{i}", [128, 8 * 1024], BF16) for i in range(2)]
            gated = ssb("fm_gated", [128, NHC * 1024], BF16)
            w2_bf = ssb("fm_w2", [128, NHC * 1024], BF16)
            wsl = [ssb(f"fm_wsl{i}", [128, 8 * 128], BF16) for i in range(4)]
            b2row = ssb("fm_b2row", [1, D], BF16)
            lngc = ssb("fm_lngc", [128, 8], F32)
            lnbc = ssb("fm_lnbc", [128, 8], BF16)
            b1c = ssb("fm_b1c", [128, 2 * NHC], F32)
            beff = ssb("fm_beff", [128, 2 * NHC], F32)
            h2 = [ssb(f"fm_h2_{i}", [128, D], F32) for i in range(3)]
            sqs = ssb("fm_sqs", [128, D], BF16)
            st = [ssb(f"fm_st{i}", [128, 8], F32) for i in range(4)]
            h2b = [ssb(f"fm_h2b{i}", [128, D], BF16) for i in range(2)]
            gsb = [ssb(f"fm_gsb{i}", [128, 1024], BF16) for i in range(2)]
            xfm = [ssb(f"fm_xfm{i}", [128, D], F32) for i in range(2)]
            xam = [ssb(f"fm_xam{i}", [128, D], F32) for i in range(2)]
            outb = xfm

            S.do("gpsimd", lambda E: E.dma_start(
                w2_bf[:, 0:21 * 1024].rearrange("p (t c) -> p t c", t=21),
                h_in[f"w2_{l}"][0:2688, :].rearrange("(t p) c -> p t c", p=128)),
                writes=[B("fm_w2")], dma="dW")
            S.do("gpsimd", lambda E: E.dma_start(
                w2_bf[0:42, 21 * 1024:22 * 1024],
                h_in[f"w2_{l}"][2688:2730, :]),
                writes=[B("fm_w2")], dma="dW")
            S.do("gpsimd", lambda E: E.dma_start(b2row[:], h_in[f"b2_{l}"]),
                 writes=[B("fm_b2row")], dma="dW")
            S.do("sync", lambda E: E.dma_start(lngc[:], h_in[f"lng{l}"]),
                 writes=[B("fm_lngc")], dma="dS")
            S.do("gpsimd", lambda E: E.dma_start(lnbc[:], h_in[f"lnb{l}"]),
                 writes=[B("fm_lnbc")], dma="dW")
            S.do("sync", lambda E: E.dma_start(b1c[:], h_in[f"b1c{l}"]),
                 writes=[B("fm_b1c")], dma="dS")

            def load_w1_slice(side, hc, sl):
                blk = (side * 22 + hc) * 128
                S.do("gpsimd", lambda E, sl=sl, blk=blk: E.dma_start(
                    wsl[sl][:], h_in[f"w1s{l}"][blk:blk + 128, :]),
                    writes=[B(f"fm_wsl{sl}")], dma="dW")

            def pre_front(cg):
                sl = cg % 3
                mg = cg // 8
                bh = B(f"fm_h2_{sl}")
                S.do("sync", lambda E, cg=cg, sl=sl: E.dma_start(
                    h2[sl][:, 0:512], xf_dram[cg * 128:(cg + 1) * 128, 0:512]),
                    reads=[B("xf_dram")], writes=[bh], dma="dS")
                if cg > 0:
                    S.do("sync", lambda E, cg=cg, sl=sl: E.dma_start(
                        h2[sl][:, 512:1024],
                        xf_dram[cg * 128 - 1:(cg + 1) * 128 - 1, 512:1024]),
                        reads=[B("xf_dram")], writes=[bh], dma="dS")
                else:
                    S.do("vector", lambda E, sl=sl: E.memset(
                        h2[sl][0:1, 512:1024], 0.0), writes=[bh])
                    S.do("sync", lambda E, sl=sl: E.dma_start(
                        h2[sl][1:128, 512:1024], xf_dram[0:127, 512:1024]),
                        reads=[B("xf_dram")], writes=[bh], dma="dS")
                bst = B(f"fm_st{sl}")
                S.do("scalar", lambda E, sl=sl: E.activation(
                    sqs[:], h2[sl][:], AF.Square, accum_out=st[sl][:, 0:1]),
                    reads=[bh], writes=[B("fm_sqs"), bst])
                S.do("vector", lambda E, sl=sl: E.reduce_sum(
                    st[sl][:, 1:2], h2[sl][:], axis=AX.X), reads=[bh], writes=[bst])
                S.do("vector", lambda E, sl=sl: E.tensor_scalar(
                    st[sl][:, 2:3], st[sl][:, 1:2], -1.0 / D, None, ALU.mult),
                    reads=[bst], writes=[bst])
                S.do("vector", lambda E, sl=sl: E.tensor_tensor(
                    st[sl][:, 3:4], st[sl][:, 2:3], st[sl][:, 2:3], ALU.mult),
                    reads=[bst], writes=[bst])
                S.do("vector", lambda E, sl=sl: E.scalar_tensor_tensor(
                    st[sl][:, 4:5], st[sl][:, 0:1], 1.0 / D, st[sl][:, 3:4],
                    ALU.mult, ALU.subtract), reads=[bst], writes=[bst])
                S.do("scalar", lambda E, sl=sl: E.activation(
                    st[sl][:, 5:6], st[sl][:, 4:5], AF.Ln, bias=cp("eps_ln")),
                    reads=[bst, B("cpool")], writes=[bst])
                S.do("scalar", lambda E, sl=sl: E.activation(
                    st[sl][:, 6:7], st[sl][:, 5:6], AF.Exp, scale=-0.5),
                    reads=[bst], writes=[bst])
                S.do("vector", lambda E, sl=sl: E.tensor_tensor(
                    st[sl][:, 7:8], st[sl][:, 2:3], st[sl][:, 6:7], ALU.mult),
                    reads=[bst], writes=[bst])
                hp = cg % 2
                bb = B(f"fm_h2b{hp}")
                S.do("scalar", lambda E, sl=sl, hp=hp: E.activation(
                    h2b[hp][:], h2[sl][:], AF.Identity, scale=st[sl][:, 6:7],
                    bias=st[sl][:, 7:8]), reads=[bh, bst], writes=[bb])

            def pre_back(cg):
                mg = cg // 8
                hp = cg % 2
                bb = B(f"fm_h2b{hp}")
                cc = cg % 8
                h2nT3 = h2nT[mg][:].rearrange("p (t n) -> p t n", t=8)
                for grp in range(2):
                    pb = 6 + grp
                    for j in range(4):
                        ic = grp * 4 + j
                        S.do("tensor", lambda E, ic=ic, j=j, pb=pb, hp=hp: E.transpose(
                            PBbf[pb][:, j * 128:(j + 1) * 128],
                            h2b[hp][:, ic * 128:(ic + 1) * 128],
                            ident[:]), reads=[bb, B("c_ident")], writes=[B(f"pb{pb}")])
                    S.do("vector", lambda E, grp=grp, pb=pb, cc=cc, h2nT3=h2nT3: E.tensor_tensor(
                        h2nT3[:, grp * 4:(grp + 1) * 4, cc * 128:(cc + 1) * 128],
                        PBbf[pb][:, 0:512].rearrange("p (t n) -> p t n", t=4),
                        lngc[:, grp * 4:(grp + 1) * 4].unsqueeze(2).broadcast_to(
                            [128, 4, 128]), ALU.mult),
                        reads=[B(f"pb{pb}"), B("fm_lngc")], writes=[B(f"fm_h2nT{mg}")])

            def w1_phase(mega):
                for hc in range(NHC):
                    cw = 42 if hc == 21 else 128
                    bank4 = (0, 1, 2, 3) if hc % 2 == 0 else (4, 5, 6, 7)
                    for side in (0, 1):
                        sl = (hc * 2 + side) % 4
                        load_w1_slice(side, hc, sl)
                        pA, pB = bank4[0:2] if side == 0 else bank4[2:4]

                        def w1grp(E, sl=sl, cw=cw, pA=pA, pB=pB, mega=mega):
                            last = None
                            for kc in range(8):
                                lhs = wsl[sl][:, kc * 128:kc * 128 + cw]
                                E.matmul(PB[pA][0:cw, :], lhs,
                                         h2nT[mega][:, kc * 1024: kc * 1024 + 512],
                                         start=(kc == 0), stop=(kc == 7))
                                last = E.matmul(PB[pB][0:cw, :], lhs,
                                                h2nT[mega][:, kc * 1024 + 512: kc * 1024 + 1024],
                                                start=(kc == 0), stop=(kc == 7))
                            return last
                        S.do("tensor", w1grp,
                             reads=[B(f"fm_wsl{sl}"), B(f"fm_h2nT{mega}")],
                             writes=[B(f"pb{pA}"), B(f"pb{pB}")])
                        if mega == 0:
                            bb_ = 7 if hc % 2 == 0 else 3

                            def bgrp(E, sl=sl, cw=cw, bb_=bb_):
                                last = None
                                for kc in range(8):
                                    last = E.matmul(
                                        PB[bb_][0:cw, 0:1],
                                        wsl[sl][:, kc * 128:kc * 128 + cw],
                                        lnbc[:, kc:kc + 1],
                                        start=(kc == 0), stop=(kc == 7))
                                return last
                            S.do("tensor", bgrp,
                                 reads=[B(f"fm_wsl{sl}"), B("fm_lnbc")],
                                 writes=[B(f"pb{bb_}")])
                            S.do("vector", lambda E, cw=cw, side=side, hc=hc, bb_=bb_: E.tensor_tensor(
                                beff[0:cw, side * NHC + hc: side * NHC + hc + 1],
                                PB[bb_][0:cw, 0:1],
                                b1c[0:cw, side * NHC + hc: side * NHC + hc + 1],
                                ALU.add),
                                reads=[B(f"pb{bb_}"), B("fm_b1c")], writes=[B("fm_beff")])
                    cw = 42 if hc == 21 else 128
                    hp = hc % 2
                    for half in range(2):
                        ga, gg = bank4[half], bank4[2 + half]
                        S.do("scalar", lambda E, hc=hc, cw=cw, gg=gg, half=half, hp=hp: E.activation(
                            gsb[hp][0:cw, half * 512:(half + 1) * 512],
                            PB[gg][0:cw, :], AF.Gelu,
                            bias=beff[0:cw, NHC + hc: NHC + hc + 1]),
                            reads=[B(f"pb{gg}"), B("fm_beff")],
                            writes=[B(f"fm_gsb{hp}")])
                        S.do("vector", lambda E, hc=hc, cw=cw, ga=ga, half=half, hp=hp: E.scalar_tensor_tensor(
                            gated[0:cw, hc * 1024 + half * 512: hc * 1024 + (half + 1) * 512],
                            PB[ga][0:cw, :],
                            beff[0:cw, hc:hc + 1],
                            gsb[hp][0:cw, half * 512:(half + 1) * 512],
                            ALU.add, ALU.mult),
                            reads=[B(f"pb{ga}"), B("fm_beff"), B(f"fm_gsb{hp}")],
                            writes=[B("fm_gated")])

            def w2_phase(mega):
                for cc in range(8):
                    cg = mega * 8 + cc
                    sl = cc % 2
                    S.do("sync", lambda E, cg=cg, sl=sl: E.dma_start(
                        xfm[sl][:], xf_dram[cg * 128:(cg + 1) * 128, :]),
                        reads=[B("xf_dram")], writes=[B(f"fm_xfm{sl}")], dma="dS")
                    if final:
                        S.do("sync", lambda E, cg=cg, sl=sl: E.dma_start(
                            xam[sl][:], xA_dram[cg * 128:(cg + 1) * 128, :]),
                            reads=[B("xA_done")], writes=[B(f"fm_xam{sl}")], dma="dS")

                    def w2grp(E, cc=cc):
                        for hc in range(NHC):
                            cw = 42 if hc == 21 else 128
                            lhs = gated[0:cw, hc * 1024 + cc * 128: hc * 1024 + (cc + 1) * 128]
                            E.matmul(PB[4][:], lhs,
                                     w2_bf[0:cw, hc * 1024: hc * 1024 + 512],
                                     start=(hc == 0), stop=False)
                            E.matmul(PB[5][:], lhs,
                                     w2_bf[0:cw, hc * 1024 + 512: hc * 1024 + 1024],
                                     start=(hc == 0), stop=False)
                        E.matmul(PB[4][:], ones_row[:], b2row[:, 0:512],
                                 start=False, stop=True)
                        return E.matmul(PB[5][:], ones_row[:], b2row[:, 512:1024],
                                        start=False, stop=True)
                    S.do("tensor", w2grp,
                         reads=[B("fm_gated"), B("fm_w2"), B("fm_b2row"),
                                B("ones_row")],
                         writes=[B("pb4"), B("pb5")])
                    bo_ = B(f"fm_xfm{sl}")
                    for half, pb in ((0, 4), (1, 5)):
                        S.do("vector", lambda E, sl=sl, half=half, pb=pb: E.tensor_tensor(
                            outb[sl][:, half * 512:(half + 1) * 512], PB[pb][:],
                            xfm[sl][:, half * 512:(half + 1) * 512], ALU.add),
                            reads=[B(f"pb{pb}")], writes=[bo_])
                    if final:
                        for half in range(2):
                            S.do("vector", lambda E, sl=sl, half=half: E.tensor_tensor(
                                outb[sl][:, half * 512:(half + 1) * 512],
                                outb[sl][:, half * 512:(half + 1) * 512],
                                xam[sl][:, half * 512:(half + 1) * 512], ALU.add),
                                reads=[bo_, B(f"fm_xam{sl}")], writes=[bo_])
                        S.do("gpsimd", lambda E, cg=cg, sl=sl: E.dma_start(
                            out_h[cg * 128:(cg + 1) * 128, :], outb[sl][:]),
                            reads=[bo_], dma="dG")
                    else:
                        S.do("gpsimd", lambda E, cg=cg, sl=sl: E.dma_start(
                            xf2_dram[cg * 128:(cg + 1) * 128, :], outb[sl][:]),
                            reads=[bo_], writes=[B("xf2_dram")], dma="dG")
                        srcT3 = srcT[:].rearrange("p (t n) -> p t n", t=8)
                        for grp, pb, eng in ((0, 0, "vector"), (1, 1, "scalar")):
                            for j in range(4):
                                ic = grp * 4 + j
                                S.do("tensor", lambda E, ic=ic, j=j, pb=pb, sl=sl: E.transpose(
                                    PB[pb][:, j * 128:(j + 1) * 128],
                                    outb[sl][:, ic * 128:(ic + 1) * 128],
                                    ident_f[:]), reads=[bo_, B("c_ident_f")],
                                    writes=[B(f"pb{pb}")])
                            S.do(eng, lambda E, cg=cg, grp=grp, pb=pb, srcT3=srcT3: COPY(
                                E, srcT3[:, grp * 4:(grp + 1) * 4, cg * 128:(cg + 1) * 128],
                                PB[pb][:, 0:512].rearrange("p (t n) -> p t n", t=4)),
                                reads=[B(f"pb{pb}")], writes=[B("srcT")])

            def load8(cc):
                hp = cc % 2
                S.do("sync", lambda E, cc=cc, hp=hp: E.dma_start(
                    h2b[hp][:], hb_dram[cc * 128:(cc + 1) * 128, :]),
                    reads=[B("hb_dram")], writes=[B(f"fm_h2b{hp}")], dma="dS")

            def prepass_group(cg0):
                pre_front(cg0)
                for cc in range(8):
                    if cc + 1 < 8:
                        pre_front(cg0 + cc + 1)
                    pre_back(cg0 + cc)

            load8(0)
            for cc in range(8):
                if cc + 1 < 8:
                    load8(cc + 1)
                pre_back(cc)
            w1_phase(0)
            prepass_group(8)
            w2_phase(0)
            w1_phase(1)
            w2_phase(1)

        prep()
        with ExitStack() as sc:
            s0f, s0b, s0xin = stage0(sc)
            stageA(sc, s0f, s0b, s0xin)
            S.barrier()
        bufs["xA_done"] = Buf("xA_done")
        bufs["xA_done"].w = {f"dG{i}": (S.cnt[f"dG{i}"], "dma")
                             for i in range(S.DMA_POOL)}
        bufs["base_done"] = Buf("base_done")
        bufs["base_done"].w = dict(bufs["xA_done"].w)
        with ExitStack() as sc:
            stageFa(sc, 0)
            S.barrier()
        with ExitStack() as sc:
            stageFm(sc, 0)
            S.barrier()
        with ExitStack() as sc:
            stageFa(sc, 1)
            S.barrier()
        with ExitStack() as sc:
            stageFm(sc, 1)
            S.barrier()

        if "xnT" in DBG:
            d = nc.dram_tensor("dbg_xnT", [128, 8 * N], F32,
                               kind="ExternalOutput").ap()
            dtmp = sb("dbg_tmp", [128, N], F32)
            for t in range(8):
                bt = B("dbg_tmp")
                S.do("vector", lambda E, t=t: E.tensor_copy(
                    dtmp[:], srcT[:, t * N:(t + 1) * N]),
                    reads=[B("srcT")], writes=[bt])
                S.do("sync", lambda E, t=t: E.dma_start(
                    d[:, t * N:(t + 1) * N], dtmp[:]), reads=[bt], dma="dS")

        for qn in ("dS", "dG", "dW", "dT"):
            S.dma_flush("sync", qn)

        block = ctx.enter_context(nc.Block())
        block.sync(lambda E: [f(E) for f in S.q["sync"]])
        block.vector(lambda E: [f(E) for f in S.q["vector"]])
        block.scalar(lambda E: [f(E) for f in S.q["scalar"]])
        block.tensor(lambda E: [f(E) for f in S.q["tensor"]])
        block.gpsimd(lambda E: [f(E) for f in S.q["gpsimd"]])

    return nc


def prep_inputs(L0_attn_x, L0_attn_omega, L0_attn_wq, L0_attn_wk, L0_attn_wv,
                L0_attn_wo, L0_attn_bo, L0_ffn_wqkv, L0_ffn_q_scale,
                L0_ffn_k_scale, L0_ffn_wo, L0_ffn_ln_g, L0_ffn_ln_b,
                L0_ffn_w1, L0_ffn_b1, L0_ffn_w2, L0_ffn_b2):
    f32 = lambda a: np.ascontiguousarray(np.asarray(a), dtype=np.float32)
    W = {}
    W["wqT"] = f32(np.asarray(L0_attn_wq).T)
    W["wkT"] = f32(np.asarray(L0_attn_wk).T)
    W["wvT"] = f32(np.asarray(L0_attn_wv).T)
    W["woT"] = f32(np.asarray(L0_attn_wo).T)
    omT = f32(np.asarray(L0_attn_omega).T)
    zz = np.zeros((64, 64), np.float32)
    W["om_top"] = f32(np.concatenate([omT, zz], axis=0))
    W["om_bot"] = f32(np.concatenate([zz, omT], axis=0))
    W["bo"] = f32(np.asarray(L0_attn_bo).reshape(1, D))
    for l in range(2):
        W[f"wqkv{l}"] = f32(np.asarray(L0_ffn_wqkv)[l])
        W[f"wof{l}"] = f32(np.asarray(L0_ffn_wo)[l])
        w1l = np.asarray(L0_ffn_w1)[l].astype(np.float32)
        w1p = np.zeros((D, 2, 22 * 128), np.float32)
        w1p[:, 0, 0:HID] = w1l[:, 0:HID]
        w1p[:, 1, 0:HID] = w1l[:, HID:2 * HID]
        W[f"w1s{l}"] = f32(w1p.reshape(8, 128, 2, 22, 128)
                           .transpose(2, 3, 1, 0, 4).reshape(44 * 128, D))
        W[f"b1_{l}"] = f32(np.asarray(L0_ffn_b1)[l].reshape(1, HID2))
        b1l = np.asarray(L0_ffn_b1)[l].astype(np.float32)
        bpad = np.zeros((2, 22 * 128), np.float32)
        bpad[0, 0:HID] = b1l[0:HID]
        bpad[1, 0:HID] = b1l[HID:2 * HID]
        W[f"b1c{l}"] = f32(np.concatenate(
            [bpad[0].reshape(22, 128).T, bpad[1].reshape(22, 128).T], axis=1))
        W[f"w2_{l}"] = f32(np.asarray(L0_ffn_w2)[l])
        W[f"b2_{l}"] = f32(np.asarray(L0_ffn_b2)[l].reshape(1, D))
        W[f"lng{l}"] = f32(np.asarray(L0_ffn_ln_g)[l].reshape(8, 128).T)
        W[f"lnb{l}"] = f32(np.asarray(L0_ffn_ln_b)[l].reshape(8, 128).T)
        W[f"qs{l}"] = f32(np.asarray(L0_ffn_q_scale)[l].reshape(1, FD))
        W[f"ks{l}"] = f32(np.asarray(L0_ffn_k_scale)[l].reshape(1, FD))
    W.update(host_consts())
    x = f32(L0_attn_x)
    return W, x


def kernel(**inputs):
    W, x = prep_inputs(**inputs)
    nc = build()
    in_maps = [dict(W, x=np.ascontiguousarray(x[b])) for b in range(8)]
    res = run_bass_kernel_spmd(nc, in_maps, core_ids=list(range(8)))
    out = np.stack([np.asarray(r["out"]) for r in res.results], axis=0)
    return out.astype(np.float32)


# version 58
# speedup vs baseline: 1.1765x; 1.1765x over previous
import math
import numpy as np
import ml_dtypes
from contextlib import ExitStack

import concourse.bass as bass
import concourse.mybir as mybir
from concourse.bass_utils import run_bass_kernel_spmd

F32 = mybir.dt.float32
BF16 = mybir.dt.bfloat16
AF = mybir.ActivationFunctionType
ALU = mybir.AluOpType
AX = mybir.AxisListType

N = 2048
D = 1024
NCH = 16
PH = 16
PD = 64
PM = 64
FH = 8
FD = 64
FI = 512
WIN = 512
HID = 2730
HID2 = 5460
LN8 = math.log(8.0)

ENGS = ("sync", "vector", "scalar", "tensor", "gpsimd")


def COPY(E, out, in_):
    return E.copy(out, in_) if hasattr(E, "copy") and not hasattr(E, "tensor_copy") \
        else E.tensor_copy(out, in_)

DBG = set()
SA_HALVES = 2
SA_P2 = 16
SA_STEP = 99


def host_tables():
    half = np.arange(0, FD, 2, dtype=np.float32)
    inv_freq = (1.0 / (10000.0 ** (half / FD))).astype(np.float32)
    t = np.arange(N, dtype=np.float32)
    fr = t[:, None] * inv_freq[None, :]
    freqs = np.concatenate([fr, fr], 1).astype(np.float32)
    base = ((half + 0.4 * FD) / (1.4 * FD)).astype(np.float32)
    power = ((t - N // 2) / float(WIN // 2)).astype(np.float32)
    scale = (base[None, :] ** power[:, None]).astype(np.float32)
    scale = np.concatenate([scale, scale], 1).astype(np.float32)
    cos = np.cos(freqs).astype(np.float32)
    sin = np.sin(freqs).astype(np.float32)
    QK = np.float32(8.0)
    tcq = (cos * scale) * QK
    tsq = (sin * scale) * QK
    tck = cos / scale
    tsk = sin / scale
    for tt in (tsq, tsk):
        tt[:, :32] *= np.float32(-1.0)

    def chunkmajor(a):
        return np.ascontiguousarray(
            a.reshape(NCH, 128, FD).transpose(1, 0, 2).reshape(128, NCH * FD)
        ).astype(np.float32)

    return {"tcq": chunkmajor(tcq), "tsq": chunkmajor(tsq),
            "tck": chunkmajor(tck), "tsk": chunkmajor(tsk)}


def host_consts():
    c = {}
    c["ident"] = np.eye(128, dtype=ml_dtypes.bfloat16)
    c["ident_f"] = np.eye(128, dtype=np.float32)
    tri_u = np.triu(np.ones((128, 128), dtype=np.float32))
    c["m_up_f32"] = tri_u
    c["m_low_bf"] = np.tril(np.ones((128, 128))).astype(ml_dtypes.bfloat16)
    c["m_up_bf"] = tri_u.astype(ml_dtypes.bfloat16)
    ones128 = np.ones((128, 128), np.float32)
    tri_l = np.tril(np.ones((128, 128), np.float32))
    c["m_cfull"] = np.concatenate(
        [tri_l, ones128, ones128, ones128, tri_u], axis=1).astype(ml_dtypes.bfloat16)
    c["m_ctail"] = np.concatenate(
        [ones128, ones128, ones128, ones128, tri_u], axis=1).astype(ml_dtypes.bfloat16)
    c.update(host_tables())
    return c


class Buf:
    __slots__ = ("w", "r", "name")

    def __init__(self, name=""):
        self.name = name
        self.w = {}
        self.r = {}


class Sched:
    DMA_POOL = 8

    def __init__(self, nc, ctx):
        self.nc = nc
        self.q = {e: [] for e in ENGS}
        self.sems = {}
        self.cnt = {}
        self.lastwait = {e: {} for e in ENGS}
        self.dma_k = {}
        for e in ENGS:
            self.mk_sem("c_" + e, ctx)
        for qn in ("dS", "dW", "dT", "dG"):
            self.dma_k[qn] = 0
            for i in range(self.DMA_POOL):
                self.mk_sem(f"{qn}{i}", ctx)

    def mk_sem(self, name, ctx):
        self.sems[name] = ctx.enter_context(self.nc.semaphore(name))
        self.cnt[name] = 0

    def dma_flush(self, eng, qn):
        for i in range(self.DMA_POOL):
            self._wait(eng, f"{qn}{i}", self.cnt[f"{qn}{i}"])

    def barrier(self, engines=ENGS):
        for e in engines:
            for s, v in self.cnt.items():
                self._wait(e, s, v)

    def _wait(self, eng, sem, val):
        if val <= 0:
            return
        if self.lastwait[eng].get(sem, 0) >= val:
            return
        self.lastwait[eng][sem] = val
        h = self.sems[sem]
        self.q[eng].append(lambda E, h=h, v=val: E.wait_ge(h, v))

    def _wait_map(self, eng, m, war=False):
        for sem, (val, peng) in m.items():
            self._wait(eng, sem, val)

    def do(self, eng, fn, reads=(), writes=(), dma=None):
        for b in reads:
            self._wait_map(eng, b.w)
        for b in writes:
            self._wait_map(eng, b.r)
            self._wait_map(eng, b.w)
        if dma is not None:
            k = self.dma_k[dma]
            self.dma_k[dma] = k + 1
            sem = f"{dma}{k % self.DMA_POOL}"
            self._wait(eng, sem, self.cnt[sem])
            self.cnt[sem] += 16
            val = self.cnt[sem]
            h = self.sems[sem]

            def run(E, fn=fn, h=h):
                inst = fn(E)
                inst.then_inc(h, 16)
            self.q[eng].append(run)
            tok = (sem, val, "dma")
        else:
            sem = "c_" + eng
            self.cnt[sem] += 1
            val = self.cnt[sem]
            h = self.sems[sem]

            def run(E, fn=fn, h=h):
                inst = fn(E)
                inst.then_inc(h, 1)
            self.q[eng].append(run)
            tok = (sem, val, eng)
        for b in writes:
            b.w[tok[0]] = (tok[1], tok[2])
            b.r = {}
        for b in reads:
            b.r[tok[0]] = (tok[1], tok[2])
        return tok


class Defer:
    def __init__(self):
        self.items = []

    def do(self, *a, **k):
        self.items.append((a, k))


def build():
    nc = bass.Bass("TRN2", target_bir_lowering=False)

    x_h = nc.dram_tensor("x", [N, D], F32, kind="ExternalInput").ap()
    h_in = {}
    for nm, shape in [
        ("wqT", [D, D]), ("wkT", [D, D]), ("wvT", [D, D]), ("woT", [D, D]),
        ("om_top", [128, PM]), ("om_bot", [128, PM]), ("bo", [1, D]),
        ("wqkv0", [D, 3 * FI]), ("wqkv1", [D, 3 * FI]),
        ("wof0", [FI, D]), ("wof1", [FI, D]),
        ("w1s0", [44 * 128, D]), ("w1s1", [44 * 128, D]),
        ("b1_0", [1, HID2]), ("b1_1", [1, HID2]),
        ("b1c0", [128, 44]), ("b1c1", [128, 44]),
        ("w2_0", [HID, D]), ("w2_1", [HID, D]),
        ("b2_0", [1, D]), ("b2_1", [1, D]),
        ("lng0", [128, 8]), ("lng1", [128, 8]),
        ("lnb0", [128, 8]), ("lnb1", [128, 8]),
        ("qs0", [1, FD]), ("qs1", [1, FD]),
        ("ks0", [1, FD]), ("ks1", [1, FD]),
        ("ident", [128, 128]), ("ident_f", [128, 128]), ("m_up_f32", [128, 128]),
        ("m_low_bf", [128, 128]), ("m_up_bf", [128, 128]),
        ("m_cfull", [128, 640]), ("m_ctail", [128, 640]),
        ("tcq", [128, NCH * FD]), ("tsq", [128, NCH * FD]),
        ("tck", [128, NCH * FD]), ("tsk", [128, NCH * FD]),
    ]:
        dt = BF16 if nm in ("ident", "m_low_bf", "m_up_bf",
                            "m_cfull", "m_ctail") else F32
        h_in[nm] = nc.dram_tensor(nm, shape, dt, kind="ExternalInput").ap()

    out_h = nc.dram_tensor("out", [N, D], F32, kind="ExternalOutput").ap()
    xn_dram = nc.dram_tensor(
        "xn_dram", [N, D], F32,
        kind="ExternalOutput" if "xn" in DBG else "Internal").ap()
    xA_dram = nc.dram_tensor(
        "xA_dram", [N, D], F32,
        kind="ExternalOutput" if "xA" in DBG else "Internal").ap()
    xf_dram = nc.dram_tensor(
        "xf_dram", [N, D], F32,
        kind="ExternalOutput" if "xf" in DBG else "Internal").ap()
    hb_dram = nc.dram_tensor("hb_dram", [8 * 128, D], BF16, kind="Internal").ap()
    xf2_dram = nc.dram_tensor(
        "xf2_dram", [N, D], F32,
        kind="ExternalOutput" if "xfm" in DBG else "Internal").ap()

    dbg_out = {}
    for nm in DBG:
        if nm.startswith("sb:"):
            pass

    ctx = ExitStack()
    with ctx:
        def sb(name, shape, dt):
            return ctx.enter_context(nc.sbuf_tensor(name, shape, dt))

        def psb(name, shape, dt):
            return ctx.enter_context(nc.psum_tensor(name, shape, dt))

        S = Sched(nc, ctx)

        srcT = sb("srcT", [128, 8 * N], BF16)
        ident = sb("idents", [128, 128], BF16)
        ident_f = sb("idents_f", [128, 128], F32)
        m_up_f32 = sb("m_up_f32s", [128, 128], F32)
        m_low_bf = sb("m_low_bfs", [128, 128], BF16)
        m_up_bf = sb("m_up_bfs", [128, 128], BF16)
        m_cfull = sb("m_cfulls", [128, 640], BF16)
        m_ctail = sb("m_ctails", [128, 640], BF16)
        ones_row = sb("ones_row", [1, 128], BF16)
        CPOOL_VALS = {"eps_ln": 1e-5, "neg_ln8": -LN8, "zero": 0.0}
        cpool = sb("cpool", [128, len(CPOOL_VALS)], F32)
        cpool_ix = {k: i for i, k in enumerate(CPOOL_VALS)}

        def cp(name, p0=0, p1=128):
            i = cpool_ix[name]
            return cpool[p0:p1, i:i + 1]

        PB = [psb(f"PB{i}", [128, 512], F32) for i in range(8)]
        PBbf = [p.ap().bitcast(BF16) for p in PB]

        bufs = {}

        def B(name):
            if name not in bufs:
                bufs[name] = Buf(name)
            return bufs[name]

        def prep():
            for nm, dst in [("ident", ident), ("ident_f", ident_f),
                            ("m_up_f32", m_up_f32),
                            ("m_low_bf", m_low_bf), ("m_up_bf", m_up_bf),
                            ("m_cfull", m_cfull), ("m_ctail", m_ctail)]:
                S.do("sync", lambda E, d=dst, s=h_in[nm]: E.dma_start(d[:], s),
                     writes=[B("c_" + nm)], dma="dS")
            S.do("vector", lambda E: E.memset(ones_row[:], 1.0),
                 writes=[B("ones_row")])
            for k, v in CPOOL_VALS.items():
                S.do("vector", lambda E, k=k, v=v: E.memset(cp(k), float(v)),
                     writes=[B("cpool")])

        def stage0(sc):
            ssb = lambda nm, shp, dt: sc.enter_context(nc.sbuf_tensor(nm, shp, dt))
            xin = [ssb(f"s0_xin{i}", [128, D], F32) for i in range(2)]
            xnf = [ssb(f"s0_xnf{i}", [128, D], F32) for i in range(2)]
            xnb = [ssb(f"s0_xnb{i}", [128, D], BF16) for i in range(2)]
            st = [ssb(f"s0_st{i}", [128, 8], F32) for i in range(2)]

            def s0_front(c):
                sl = c % 2
                bx = B(f"s0_xin{sl}")
                S.do("sync", lambda E, sl=sl, c=c: E.dma_start(
                    xin[sl][:], x_h[c * 128:(c + 1) * 128, :]), writes=[bx], dma="dS")
                bst = B(f"s0_st{sl}")
                bxn = B(f"s0_xnf{sl}")
                S.do("scalar", lambda E, sl=sl: E.activation(
                    xnf[sl][:], xin[sl][:], AF.Square, accum_out=st[sl][:, 0:1]),
                    reads=[bx], writes=[bxn, bst])
                S.do("vector", lambda E, sl=sl: E.reduce_sum(
                    st[sl][:, 1:2], xin[sl][:], axis=AX.X), reads=[bx], writes=[bst])
                S.do("vector", lambda E, sl=sl: E.tensor_scalar(
                    st[sl][:, 2:3], st[sl][:, 1:2], 1.0 / D, None, ALU.mult),
                    reads=[bst], writes=[bst])
                S.do("vector", lambda E, sl=sl: E.tensor_tensor(
                    st[sl][:, 3:4], st[sl][:, 2:3], st[sl][:, 2:3], ALU.mult),
                    reads=[bst], writes=[bst])
                S.do("vector", lambda E, sl=sl: E.scalar_tensor_tensor(
                    st[sl][:, 4:5], st[sl][:, 0:1], 1.0 / D, st[sl][:, 3:4],
                    ALU.mult, ALU.subtract), reads=[bst], writes=[bst])
                S.do("scalar", lambda E, sl=sl: E.activation(
                    st[sl][:, 5:6], st[sl][:, 4:5], AF.Ln, bias=cp("eps_ln")),
                    reads=[bst, B("cpool")], writes=[bst])
                S.do("scalar", lambda E, sl=sl: E.activation(
                    st[sl][:, 6:7], st[sl][:, 5:6], AF.Exp, scale=-0.5),
                    reads=[bst], writes=[bst])
                S.do("vector", lambda E, sl=sl: E.tensor_tensor(
                    st[sl][:, 7:8], st[sl][:, 2:3], st[sl][:, 6:7], ALU.mult),
                    reads=[bst], writes=[bst])
                bxn = B(f"s0_xnf{sl}")
                S.do("vector", lambda E, sl=sl: E.tensor_scalar(
                    xnf[sl][:], xin[sl][:], st[sl][:, 6:7], st[sl][:, 7:8],
                    ALU.mult, ALU.subtract), reads=[bx, bst], writes=[bxn])
                S.do("gpsimd", lambda E, sl=sl, c=c: E.dma_start(
                    xn_dram[c * 128:(c + 1) * 128, :], xnf[sl][:]),
                    reads=[bxn], dma="dG")
                bxb = B(f"s0_xnb{sl}")
                S.do("gpsimd", lambda E, sl=sl: E.dma_start(xnb[sl][:], xnf[sl][:]),
                     reads=[bxn], writes=[bxb], dma="dW")

            def s0_back(c):
                sl = c % 2
                bxb = B(f"s0_xnb{sl}")
                srcT3 = srcT[:].rearrange("p (t n) -> p t n", t=8)
                for grp in range(4):
                    pb = 6 + (grp % 2)
                    bpb = B(f"pb{pb}")
                    for j in range(2):
                        ic = grp * 2 + j
                        S.do("tensor", lambda E, sl=sl, ic=ic, j=j, pb=pb: E.transpose(
                            PBbf[pb][:, 768 + j * 128: 768 + (j + 1) * 128],
                            xnb[sl][:, ic * 128:(ic + 1) * 128],
                            ident[:]), reads=[bxb, B("c_ident")], writes=[bpb])
                    eng = "vector" if grp % 2 == 0 else "scalar"
                    S.do(eng, lambda E, c=c, grp=grp, pb=pb: COPY(
                        E, srcT3[:, grp * 2: grp * 2 + 2, c * 128:(c + 1) * 128],
                        PBbf[pb][:, 768:1024].rearrange("p (t n) -> p t n", t=2)),
                        reads=[bpb],
                        writes=[B("srcT"), B(f"srcT_h{c // 8}")])

            return s0_front, s0_back, xin

        def stageA(sc, s0_front, s0_back, s0_xin):
            ssb = lambda nm, shp, dt: sc.enter_context(nc.sbuf_tensor(nm, shp, dt))
            wsl = [ssb(f"sa_wsl{i}", [128, 8 * 128], BF16) for i in range(2)]
            wv_bf = ssb("sa_wv", [128, 8 * 1024], BF16)
            woT_bf = ssb("sa_wo", [128, 8 * 1024], BF16)
            om_t = ssb("sa_omt", [128, PM], BF16)
            om_b = ssb("sa_omb", [128, PM], BF16)
            neg_t = ssb("sa_negt", [128, PM], BF16)
            neg_b = ssb("sa_negb", [128, PM], BF16)
            bo_row = ssb("sa_bo", [1, D], BF16)
            qT_sb = ssb("sa_qT", [128, 8 * 1024], BF16)
            kT_sb = ssb("sa_kT", [128, 8 * 1024], BF16)
            vaug = ssb("sa_vaug", [128, NCH * 1040], BF16)
            phikT = [ssb(f"sa_phikT{i}", [128, PH * 128], BF16) for i in range(2)]
            phiqT = [ssb(f"sa_phiqT{i}", [128, PH * 128], BF16) for i in range(2)]
            phiktok = [ssb(f"sa_phiktok{i}", [128, 1024], BF16) for i in range(2)]
            ksqh = [ssb(f"sa_ksqh{i}", [128, 1024], BF16) for i in range(2)]
            qsqh = [ssb(f"sa_qsqh{i}", [128, 1024], BF16) for i in range(2)]
            ATm0 = ssb("sa_ATm", [128, 16 * 128], BF16)
            ATm = [ATm0, ATm0]
            Sbf = [ssb(f"sa_Sbf{i}", [128, PH * 65], BF16) for i in range(2)]
            dent = ssb("sa_dent", [128, 16], F32)
            drec = ssb("sa_drec", [128, 16], F32)
            pair = [ssb(f"sa_pair{i}", [128, 8 * 128], BF16) for i in range(2)]
            innT = [ssb(f"sa_innT{i}", [128, 8 * 128], BF16) for i in range(2)]
            xa_in = s0_xin
            xa_out = xa_in

            kT3 = kT_sb[:].rearrange("p (t c) -> p t c", t=8)
            qT3 = qT_sb[:].rearrange("p (t c) -> p t c", t=8)

            S.do("gpsimd", lambda E: E.dma_start(
                wv_bf[:].rearrange("p (t c) -> p t c", t=8),
                h_in["wvT"].rearrange("(t p) c -> p t c", p=128)),
                writes=[B("sa_wv")], dma="dW")
            S.do("gpsimd", lambda E: E.dma_start(
                woT_bf[:].rearrange("p (t c) -> p t c", t=8),
                h_in["woT"].rearrange("(t p) c -> p t c", p=128)),
                writes=[B("sa_wo")], dma="dW")
            S.do("gpsimd", lambda E: E.dma_start(om_t[:], h_in["om_top"]),
                 writes=[B("sa_om")], dma="dW")
            S.do("gpsimd", lambda E: E.dma_start(om_b[:], h_in["om_bot"]),
                 writes=[B("sa_om")], dma="dW")
            S.do("gpsimd", lambda E: E.dma_start(bo_row[:], h_in["bo"]),
                 writes=[B("sa_bo")], dma="dW")
            S.do("vector", lambda E: E.memset(neg_t[:], 0.0), writes=[B("sa_neg")])
            S.do("vector", lambda E: E.memset(neg_t[0:64, :], -1.0), writes=[B("sa_neg")])
            S.do("vector", lambda E: E.memset(neg_b[:], 0.0), writes=[B("sa_neg")])
            S.do("vector", lambda E: E.memset(neg_b[64:128, :], -1.0), writes=[B("sa_neg")])
            for i in range(2):
                S.do("vector", lambda E, i=i: E.memset(phikT[i][:], 0.0),
                     writes=[B(f"sa_phikT{i}")])
                S.do("vector", lambda E, i=i: E.memset(phiqT[i][:], 0.0),
                     writes=[B(f"sa_phiqT{i}")])
                S.do("vector", lambda E, i=i: E.memset(Sbf[i][:], 0.0),
                     writes=[B(f"sa_Sbf{i}")])
            vaug4 = vaug[:].rearrange("p (c h e) -> p c h e", c=NCH, h=PH)
            S.do("vector", lambda E: E.memset(vaug4[:, :, :, 64:65], 1.0),
                 writes=[B("sa_vaug_ones")])

            def load_wslice(mat, ic, sl, Q):
                Q.do("gpsimd", lambda E: E.dma_start(
                    wsl[sl][:].rearrange("p (t c) -> p t c", t=8),
                    h_in[mat].rearrange("(t p) c -> p t c", p=128)[:, :, ic * 128:(ic + 1) * 128]),
                    reads=[], writes=[B(f"sa_wsl{sl}")], dma="dW")

            def p1_half(half, Q):
                t0 = half * 1024
                bsrc = B(f"srcT_h{half}")
                for mat, dst, dstb in (("wqT", qT_sb, "sa_qT"), ("wkT", kT_sb, "sa_kT")):
                    for ic in range(8):
                        sl = ic % 2
                        load_wslice(mat, ic, sl, Q)
                        pA, pB = (0, 1) if ic % 2 == 0 else (2, 3)

                        def mmgrp(E, sl=sl, pA=pA, pB=pB, t0=t0):
                            last = None
                            for kc in range(8):
                                lhs = wsl[sl][:, kc * 128:(kc + 1) * 128]
                                E.matmul(PB[pA][:], lhs,
                                         srcT[:, kc * N + t0: kc * N + t0 + 512],
                                         start=(kc == 0), stop=(kc == 7))
                                last = E.matmul(PB[pB][:], lhs,
                                                srcT[:, kc * N + t0 + 512: kc * N + t0 + 1024],
                                                start=(kc == 0), stop=(kc == 7))
                            return last
                        Q.do("tensor", mmgrp,
                             reads=[B(f"sa_wsl{sl}"), bsrc],
                             writes=[B(f"pb{pA}"), B(f"pb{pB}")])
                        Q.do("vector", lambda E, ic=ic, pA=pA, dst=dst: E.tensor_copy(
                            dst[:, ic * 1024: ic * 1024 + 512], PB[pA][:]),
                            reads=[B(f"pb{pA}")], writes=[B(dstb)])
                        Q.do("scalar", lambda E, ic=ic, pB=pB, dst=dst: E.copy(
                            dst[:, ic * 1024 + 512: ic * 1024 + 1024], PB[pB][:]),
                            reads=[B(f"pb{pB}")], writes=[B(dstb)])
            def vproj(cg):
                def vgrp(E, cg=cg):
                    last = None
                    for kc in range(8):
                        lhs = srcT[:, kc * N + cg * 128: kc * N + (cg + 1) * 128]
                        E.matmul(PB[4][:], lhs, wv_bf[:, kc * 1024: kc * 1024 + 512],
                                 start=(kc == 0), stop=(kc == 7))
                        last = E.matmul(PB[5][:], lhs,
                                        wv_bf[:, kc * 1024 + 512: kc * 1024 + 1024],
                                        start=(kc == 0), stop=(kc == 7))
                    return last
                S.do("tensor", vgrp, reads=[B("sa_wv"), B(f"srcT_h{cg // 8}")],
                     writes=[B("pb4"), B("pb5")])
                for bank, h0, eng in ((4, 0, "vector"), (5, 8, "scalar")):
                    S.do(eng, lambda E, cg=cg, bank=bank, h0=h0: COPY(
                        E, vaug4[:, cg, h0:h0 + 8, 0:64],
                        PB[bank][:].rearrange("p (h e) -> p h e", h=8)),
                        reads=[B(f"pb{bank}")], writes=[B("sa_vaug")])

            def sb_off(h):
                return (0 if h < 8 else 260) + ((h // 2) % 4) * 65

            def p2_front(c, Q):
                tc = (c % 8) * 128
                sl = c % 2
                P = lambda nm: B(f"{nm}{sl}")
                Q.do("scalar", lambda E, sl=sl, tc=tc: E.activation(
                    ksqh[sl][:], kT3[:, :, tc:tc + 128], AF.Square,
                    scale=0.7071067811865476),
                    reads=[B("sa_kT")], writes=[P("sa_ksqh")])
                Q.do("scalar", lambda E, sl=sl, tc=tc: E.activation(
                    qsqh[sl][:], qT3[:, :, tc:tc + 128], AF.Square,
                    scale=0.7071067811865476),
                    reads=[B("sa_qT")], writes=[P("sa_qsqh")])
                for (nm, srcb, sq, dst, dstb) in (
                        ("fkT", "sa_kT", ksqh[sl], phikT[sl], f"sa_phikT{sl}"),
                        ("fqT", "sa_qT", qsqh[sl], phiqT[sl], f"sa_phiqT{sl}")):
                    srcT3 = kT3 if srcb == "sa_kT" else qT3
                    sqb = f"sa_ksqh{sl}" if srcb == "sa_kT" else f"sa_qsqh{sl}"
                    for bi, bank in enumerate((0, 1)):
                        def fgrp(E, bi=bi, bank=bank, srcT3=srcT3, sq=sq, tc=tc):
                            E.matmul(PB[bank][0:64, :], om_t[:],
                                     srcT3[:, bi * 4:(bi + 1) * 4, tc:tc + 128],
                                     start=True, stop=False)
                            E.matmul(PB[bank][0:64, :], neg_t[:],
                                     sq[:, bi * 512:(bi + 1) * 512],
                                     start=False, stop=True)
                            E.matmul(PB[bank][64:128, :], om_b[:],
                                     srcT3[:, bi * 4:(bi + 1) * 4, tc:tc + 128],
                                     start=True, stop=False)
                            return E.matmul(PB[bank][64:128, :], neg_b[:],
                                            sq[:, bi * 512:(bi + 1) * 512],
                                            start=False, stop=True)
                        Q.do("tensor", fgrp,
                             reads=[B(srcb), B(sqb), B("sa_om"), B("sa_neg")],
                             writes=[B(f"pb{bank}")])
                        phz = dst[:].rearrange("p (t q c) -> p t q c", t=8, q=2)
                        Q.do("scalar", lambda E, bi=bi, bank=bank, phz=phz: E.activation(
                            phz[0:64, 4 * bi:4 * bi + 4, 0, :],
                            PB[bank][0:64, :].rearrange("p (t c) -> p t c", t=4),
                            AF.Exp, bias=cp("neg_ln8", 0, 64)),
                            reads=[B(f"pb{bank}"), B("cpool")], writes=[B(dstb)])
                        Q.do("scalar", lambda E, bi=bi, bank=bank, phz=phz: E.activation(
                            phz[64:128, 4 * bi:4 * bi + 4, 1, :],
                            PB[bank][64:128, :].rearrange("p (t c) -> p t c", t=4),
                            AF.Exp, bias=cp("neg_ln8", 64, 128)),
                            reads=[B(f"pb{bank}"), B("cpool")], writes=[B(dstb)])
                for bank, h0 in ((0, 0), (1, 8)):
                    def tgrp(E, bank=bank, h0=h0, tc=tc, sl=sl):
                        last = None
                        for i in range(8):
                            h = h0 + i
                            t = h // 2
                            omx = om_t if h % 2 == 0 else om_b
                            ngx = neg_t if h % 2 == 0 else neg_b
                            E.matmul(PB[bank][:, i * 64:(i + 1) * 64],
                                     kT3[:, t, tc:tc + 128], omx[:],
                                     start=True, stop=False)
                            last = E.matmul(PB[bank][:, i * 64:(i + 1) * 64],
                                            ksqh[sl][:, t * 128:(t + 1) * 128], ngx[:],
                                            start=False, stop=True)
                        return last
                    Q.do("tensor", tgrp,
                         reads=[B("sa_kT"), P("sa_ksqh"), B("sa_om"), B("sa_neg")],
                         writes=[B(f"pb{bank}")])
                    Q.do("scalar", lambda E, bank=bank, h0=h0, sl=sl: E.activation(
                        phiktok[sl][:, h0 * 64:(h0 + 8) * 64], PB[bank][:], AF.Exp,
                        bias=cp("neg_ln8")),
                        reads=[B(f"pb{bank}"), B("cpool")], writes=[P("sa_phiktok")])

            def p2_back(c, dq):
                sl = c % 2
                P = lambda nm: B(f"{nm}{sl}")

                def drain(n=3):
                    for _ in range(n):
                        if dq:
                            a, k = dq.pop(0)
                            S.do(*a, **k)

                if c > 0:
                    sbz = Sbf[sl][:].rearrange("p (t q e) -> p t q e", t=8, q=2)
                    for bank, t0 in ((6, 0), (7, 4)):
                        S.do("scalar", lambda E, bank=bank, t0=t0, sbz=sbz: E.copy(
                            sbz[0:64, t0:t0 + 4, 0, :],
                            PB[bank][0:64, 0:260].rearrange("p (s e) -> p s e", s=4)),
                            reads=[B(f"pb{bank}")], writes=[P("sa_Sbf")])
                        S.do("scalar", lambda E, bank=bank, t0=t0, sbz=sbz: E.copy(
                            sbz[64:128, t0:t0 + 4, 1, :],
                            PB[bank][64:128, 0:260].rearrange("p (s e) -> p s e", s=4)),
                            reads=[B(f"pb{bank}")], writes=[P("sa_Sbf")])
                drain()
                for g in range(4):
                    bank = 2 + g % 2

                    def agrp(E, g=g, sl=sl, bank=bank):
                        last = None
                        for i in range(4):
                            h = 4 * g + i
                            last = E.matmul(PB[bank][:, i * 128:(i + 1) * 128],
                                            phikT[sl][:, h * 128:(h + 1) * 128],
                                            phiqT[sl][:, h * 128:(h + 1) * 128],
                                            start=True, stop=True)
                        return last
                    S.do("tensor", agrp, reads=[P("sa_phikT"), P("sa_phiqT")],
                         writes=[B(f"pb{bank}")])
                    S.do("vector", lambda E, g=g, sl=sl, bank=bank: E.tensor_tensor(
                        ATm[sl][:, g * 512:(g + 1) * 512].rearrange(
                            "p (i c) -> p i c", i=4),
                        PB[bank][:].rearrange("p (i c) -> p i c", i=4),
                        m_up_f32[:].unsqueeze(1).broadcast_to([128, 4, 128]),
                        ALU.mult),
                        reads=[B(f"pb{bank}"), B("c_m_up_f32")], writes=[B("sa_ATm")])
                    drain()
                for g in range(4):
                    bank = 2 + g % 2

                    def ogrp(E, g=g, c=c, sl=sl, bank=bank):
                        last = None
                        for i in range(4):
                            h = 4 * g + i
                            last = E.matmul(PB[bank][:, i * 65: i * 65 + 65],
                                            ATm[sl][:, h * 128:(h + 1) * 128],
                                            vaug[:, c * 1040 + h * 65: c * 1040 + h * 65 + 65],
                                            start=True, stop=(c == 0))
                            if c > 0:
                                last = E.matmul(PB[bank][:, i * 65: i * 65 + 65],
                                                phiqT[sl][:, h * 128:(h + 1) * 128],
                                                Sbf[sl][:, h * 65: h * 65 + 65],
                                                start=False, stop=True)
                        return last
                    S.do("tensor", ogrp,
                         reads=[B("sa_ATm"), P("sa_phiqT"), B("sa_vaug"),
                                B("sa_vaug_ones"), P("sa_Sbf")],
                         writes=[B(f"pb{bank}")])
                    pbg3 = PB[bank][:, 0:260].rearrange("p (i e) -> p i e", i=4)
                    S.do("vector", lambda E, g=g, pbg3=pbg3: E.tensor_scalar(
                        dent[:, g * 4:(g + 1) * 4].rearrange("p (i e) -> p i e", e=1),
                        pbg3[:, :, 64:65], 1e-6, None, ALU.add),
                        reads=[B(f"pb{bank}")], writes=[B("sa_dent")])
                    S.do("vector", lambda E, g=g: E.reciprocal(
                        drec[:, g * 4:(g + 1) * 4], dent[:, g * 4:(g + 1) * 4]),
                        reads=[B("sa_dent")], writes=[B("sa_drec")])
                    S.do("vector", lambda E, g=g, pbg3=pbg3, sl=sl: E.scalar_tensor_tensor(
                        pair[sl][:, g * 256:(g + 1) * 256].rearrange(
                            "p (i c) -> p i c", i=4),
                        pbg3[:, :, 0:64], 1.0,
                        drec[:, g * 4:(g + 1) * 4].unsqueeze(2).broadcast_to(
                            [128, 4, 64]),
                        ALU.mult, ALU.mult),
                        reads=[B(f"pb{bank}"), B("sa_drec")], writes=[P("sa_pair")])
                    drain()
                for bank, h0 in ((6, 0), (7, 8)):
                    def sgrp(E, bank=bank, h0=h0, c=c, sl=sl):
                        last = None
                        for i in range(8):
                            h = h0 + i
                            par = 64 * (h % 2)
                            slot = (h // 2) % 4
                            last = E.matmul(
                                PB[bank][par:par + 64, slot * 65: slot * 65 + 65],
                                phiktok[sl][:, h * 64:(h + 1) * 64],
                                vaug[:, c * 1040 + h * 65: c * 1040 + h * 65 + 65],
                                start=False, stop=True,
                                skip_group_check=True)
                        return last
                    S.do("tensor", sgrp,
                         reads=[P("sa_phiktok"), B("sa_vaug"), B("sa_vaug_ones")],
                         writes=[B(f"pb{bank}")])
                    drain()
                for grp, pb, eng in ((0, 4, "vector"), (1, 5, "scalar")):
                    for j in range(4):
                        p = grp * 4 + j
                        S.do("tensor", lambda E, p=p, j=j, pb=pb, sl=sl: E.transpose(
                            PBbf[pb][:, j * 128:(j + 1) * 128],
                            pair[sl][:, p * 128:(p + 1) * 128], ident[:]),
                            reads=[P("sa_pair"), B("c_ident")], writes=[B(f"pb{pb}")])
                    S.do(eng, lambda E, grp=grp, pb=pb, sl=sl: COPY(
                        E, innT[sl][:, grp * 512:(grp + 1) * 512], PBbf[pb][:, 0:512]),
                        reads=[B(f"pb{pb}")], writes=[P("sa_innT")])
                    drain(4)
                sl2 = c % 2
                S.do("sync", lambda E, c=c, sl=sl2: E.dma_start(
                    xa_in[sl][:], x_h[c * 128:(c + 1) * 128, :]),
                    writes=[B(f"s0_xin{sl}")], dma="dS")

                def wogrp(E, sl=sl):
                    for ic in range(8):
                        lhs = innT[sl][:, ic * 128:(ic + 1) * 128]
                        E.matmul(PB[4][:], lhs, woT_bf[:, ic * 1024: ic * 1024 + 512],
                                 start=(ic == 0), stop=False)
                        E.matmul(PB[5][:], lhs, woT_bf[:, ic * 1024 + 512: ic * 1024 + 1024],
                                 start=(ic == 0), stop=False)
                    E.matmul(PB[4][:], ones_row[:], bo_row[:, 0:512],
                             start=False, stop=True)
                    return E.matmul(PB[5][:], ones_row[:], bo_row[:, 512:1024],
                                    start=False, stop=True)
                S.do("tensor", wogrp,
                     reads=[P("sa_innT"), B("sa_wo"), B("sa_bo"), B("ones_row")],
                     writes=[B("pb4"), B("pb5")])
                S.do("vector", lambda E, sl=sl: E.tensor_tensor(
                    xa_out[sl][:, 0:512], PB[4][:], xa_in[sl][:, 0:512], ALU.add),
                    reads=[B("pb4")], writes=[P("s0_xin")])
                S.do("vector", lambda E, sl=sl: E.tensor_tensor(
                    xa_out[sl][:, 512:1024], PB[5][:], xa_in[sl][:, 512:1024], ALU.add),
                    reads=[B("pb5")], writes=[P("s0_xin")])
                S.do("gpsimd", lambda E, c=c, sl=sl: E.dma_start(
                    xA_dram[c * 128:(c + 1) * 128, :], xa_out[sl][:]),
                    reads=[P("s0_xin")], dma="dG")
                while dq:
                    a, k = dq.pop(0)
                    S.do(*a, **k)
            for half in range(min(2, SA_HALVES)):
                if half == 0:
                    s0_front(0)
                    for c in range(8):
                        s0_front(c + 1)
                        s0_back(c)
                        vproj(c)
                    dp1 = Defer()
                    p1_half(0, dp1)
                    it = dp1.items
                    per = (len(it) + 7) // 8
                    k = 0
                    for c in range(8, 16):
                        for _ in range(per):
                            if k < len(it):
                                a, kw = it[k]
                                S.do(*a, **kw)
                                k += 1
                        if c + 1 < 16:
                            s0_front(c + 1)
                        s0_back(c)
                        vproj(c)
                    while k < len(it):
                        a, kw = it[k]
                        S.do(*a, **kw)
                        k += 1
                    S.do("vector", lambda E: E.memset(PB[6][:], 0.0),
                         writes=[B("pb6")])
                    S.do("vector", lambda E: E.memset(PB[7][:], 0.0),
                         writes=[B("pb7")])
                else:
                    p1_half(half, S)
                for cc in range(8):
                    c = half * 8 + cc
                    if c >= SA_P2:
                        continue
                    if cc == 0:
                        d = Defer()
                        p2_front(c, d)
                        for a, k in d.items:
                            S.do(*a, **k)
                    dn = Defer()
                    if cc + 1 < 8 and c + 1 < SA_P2:
                        p2_front(c + 1, dn)
                    p2_back(c, dn.items)


        def stageFa(sc, l):
            ssb = lambda nm, shp, dt: sc.enter_context(
                nc.sbuf_tensor(f"l{l}" + nm, shp, dt))
            base_dram = xn_dram if l == 0 else xf2_dram
            base_buf = "base_done" if l == 0 else "xf2_dram"

            tmast = {nm: ssb("fa_tm_" + nm, [128, NCH * FD], F32)
                     for nm in ("tcq", "tsq", "tck", "tsk")}
            for nm in ("tcq", "tsq", "tck", "tsk"):
                S.do("sync", lambda E, nm=nm: E.dma_start(
                    tmast[nm][:], h_in[nm]), writes=[B("c_" + nm)], dma="dS")
            wqkv_bf = ssb("fa_wqkv", [128, 8 * 1536], BF16)
            wof_bf = ssb("fa_wof", [128, 4 * 1024], BF16)
            tq1 = ssb("fa_tq1", [128, NCH * FD], F32)
            tq2 = ssb("fa_tq2", [128, NCH * FD], F32)
            tk1 = ssb("fa_tk1", [128, NCH * FD], F32)
            tk2 = ssb("fa_tk2", [128, NCH * FD], F32)
            qsr = ssb("fa_qsr", [128, FD], F32)
            qsp = ssb("fa_qsp", [128, FD], F32)
            ksr = ssb("fa_ksr", [128, FD], F32)
            ksp = ssb("fa_ksp", [128, FD], F32)
            srow = ssb("fa_srow", [1, 2 * FD], F32)
            ones_f32 = ssb("fa_ones", [1, 128], F32)
            hsh0 = ssb("fa_hsh", [128, 4 * 128], BF16)
            hsh = [hsh0, hsh0]
            vaugF = ssb("fa_vaug", [128, NCH * 520], BF16)
            kTf = ssb("fa_kTf", [128, 4 * N], BF16)
            qTz = [ssb(f"fa_qTz{i}", [128, FH * 128], BF16) for i in range(2)]

            k_f32 = [ssb(f"fa_kf32{i}", [128, FI], F32) for i in range(2)]
            q_f32 = [ssb(f"fa_qf32{i}", [128, FI], F32) for i in range(2)]
            ssq = [ssb(f"fa_ssq{i}", [128, 2 * FH], F32) for i in range(2)]
            inv = [ssb(f"fa_inv{i}", [128, 2 * FH], F32) for i in range(2)]
            a_q = [ssb(f"fa_aq{i}", [128, FI], F32) for i in range(2)]
            b_q = [ssb(f"fa_bq{i}", [128, FI], F32) for i in range(2)]
            a_k = [ssb(f"fa_ak{i}", [128, FI], F32) for i in range(2)]
            b_k = [ssb(f"fa_bk{i}", [128, FI], F32) for i in range(2)]
            u_q = q_f32
            u_k = k_f32
            qrot = [ssb(f"fa_qrot{i}", [128, FI], BF16) for i in range(2)]
            krot = [ssb(f"fa_krot{i}", [128, FI], BF16) for i in range(2)]
            pbuf = [ssb(f"fa_pbuf{i}", [128, 640], BF16) for i in range(4)]
            pT = [ssb(f"fa_pT{i}", [128, 5 * 128], BF16) for i in range(2)]
            nmx = ssb("fa_nmx", [128, 2 * FH], F32)
            dent = ssb("fa_dent", [128, FH], F32)
            drec = ssb("fa_drec", [128, FH], F32)
            o_pair = [ssb(f"fa_opair{i}", [128, FI], BF16) for i in range(2)]
            oT = [ssb(f"fa_oT{i}", [128, 4 * 128], BF16) for i in range(2)]
            xr_in = [ssb(f"fa_xrin{i}", [128, D], F32) for i in range(2)]
            xf_out = xr_in

            vaugF4 = vaugF[:].rearrange("p (c h e) -> p c h e", c=NCH, h=FH)
            tq1c = tq1[:].rearrange("p (c e) -> p c e", c=NCH)
            tq2c = tq2[:].rearrange("p (c e) -> p c e", c=NCH)
            tk1c = tk1[:].rearrange("p (c e) -> p c e", c=NCH)
            tk2c = tk2[:].rearrange("p (c e) -> p c e", c=NCH)

            S.do("gpsimd", lambda E: E.dma_start(
                wqkv_bf[:].rearrange("p (t c) -> p t c", t=8),
                h_in[f"wqkv{l}"].rearrange("(t p) c -> p t c", p=128)),
                writes=[B("fa_wqkv")], dma="dW")
            S.do("gpsimd", lambda E: E.dma_start(
                wof_bf[:].rearrange("p (t c) -> p t c", t=4),
                h_in[f"wof{l}"].rearrange("(t p) c -> p t c", p=128)),
                writes=[B("fa_wof")], dma="dW")
            S.do("sync", lambda E: E.dma_start(srow[:, 0:FD], h_in[f"qs{l}"]),
                 writes=[B("fa_srow")], dma="dS")
            S.do("sync", lambda E: E.dma_start(srow[:, FD:2 * FD], h_in[f"ks{l}"]),
                 writes=[B("fa_srow")], dma="dS")
            S.do("vector", lambda E: E.memset(ones_f32[:], 1.0), writes=[B("fa_ones")])
            S.do("vector", lambda E: E.memset(qTz[0][:], 0.0), writes=[B("fa_qTz0")])
            S.do("vector", lambda E: E.memset(qTz[1][:], 0.0), writes=[B("fa_qTz1")])
            S.do("vector", lambda E: E.memset(vaugF4[:, :, :, 64:65], 1.0),
                 writes=[B("fa_vaug_ones")])
            S.do("tensor", lambda E: E.matmul(
                PB[0][:, 0:2 * FD], ones_f32[:], srow[:], start=True, stop=True),
                reads=[B("fa_srow"), B("fa_ones")], writes=[B("pb0")])
            S.do("vector", lambda E: E.tensor_copy(qsr[:], PB[0][:, 0:FD]),
                 reads=[B("pb0")], writes=[B("fa_qsr")])
            S.do("vector", lambda E: E.tensor_copy(ksr[:], PB[0][:, FD:2 * FD]),
                 reads=[B("pb0")], writes=[B("fa_ksr")])
            for rep, per, brep, bper in ((qsr, qsp, "fa_qsr", "fa_qsp"),
                                         (ksr, ksp, "fa_ksr", "fa_ksp")):
                S.do("vector", lambda E, rep=rep, per=per: E.tensor_copy(
                    per[:, 0:32], rep[:, 32:64]), reads=[B(brep)], writes=[B(bper)])
                S.do("vector", lambda E, rep=rep, per=per: E.tensor_copy(
                    per[:, 32:64], rep[:, 0:32]), reads=[B(brep)], writes=[B(bper)])
            for dst, bd, mast, rep, brep in (
                    (tq1, "fa_tq1", "tcq", qsr, "fa_qsr"),
                    (tq2, "fa_tq2", "tsq", qsp, "fa_qsp"),
                    (tk1, "fa_tk1", "tck", ksr, "fa_ksr"),
                    (tk2, "fa_tk2", "tsk", ksp, "fa_ksp")):
                S.do("vector", lambda E, dst=dst, mast=mast, rep=rep: E.tensor_tensor(
                    dst[:].rearrange("p (c e) -> p c e", c=NCH),
                    tmast[mast][:].rearrange("p (c e) -> p c e", c=NCH),
                    rep[:].unsqueeze(1).broadcast_to([128, NCH, FD]),
                    ALU.mult),
                    reads=[B("c_" + mast), B(brep)], writes=[B(bd)])

            pfh = [ssb(f"fa_pfh{i}", [128, D], F32) for i in range(2)]
            pfst = [ssb(f"fa_pfst{i}", [128, 8], F32) for i in range(2)]
            pfb = [ssb(f"fa_pfb{i}", [128, D], BF16) for i in range(2)]

            def pf_front(cg, Q):
                sl = cg % 2
                bh = B(f"fa_pfh{sl}")
                bb = B(f"fa_pfb{sl}")
                bst = B(f"fa_pfst{sl}")
                Q.do("sync", lambda E, cg=cg, sl=sl: E.dma_start(
                    pfh[sl][:, 0:512], xf_dram[cg * 128:(cg + 1) * 128, 0:512]),
                    reads=[B("xf_dram")], writes=[bh], dma="dS")
                if cg > 0:
                    Q.do("sync", lambda E, cg=cg, sl=sl: E.dma_start(
                        pfh[sl][:, 512:1024],
                        xf_dram[cg * 128 - 1:(cg + 1) * 128 - 1, 512:1024]),
                        reads=[B("xf_dram")], writes=[bh], dma="dS")
                else:
                    Q.do("vector", lambda E, sl=sl: E.memset(
                        pfh[sl][0:1, 512:1024], 0.0), writes=[bh])
                    Q.do("sync", lambda E, sl=sl: E.dma_start(
                        pfh[sl][1:128, 512:1024], xf_dram[0:127, 512:1024]),
                        reads=[B("xf_dram")], writes=[bh], dma="dS")
                Q.do("scalar", lambda E, sl=sl: E.activation(
                    pfb[sl][:], pfh[sl][:], AF.Square, accum_out=pfst[sl][:, 0:1]),
                    reads=[bh], writes=[bb, bst])
                Q.do("vector", lambda E, sl=sl: E.reduce_sum(
                    pfst[sl][:, 1:2], pfh[sl][:], axis=AX.X), reads=[bh], writes=[bst])
                Q.do("vector", lambda E, sl=sl: E.tensor_scalar(
                    pfst[sl][:, 2:3], pfst[sl][:, 1:2], -1.0 / D, None, ALU.mult),
                    reads=[bst], writes=[bst])
                Q.do("vector", lambda E, sl=sl: E.tensor_tensor(
                    pfst[sl][:, 3:4], pfst[sl][:, 2:3], pfst[sl][:, 2:3], ALU.mult),
                    reads=[bst], writes=[bst])
                Q.do("vector", lambda E, sl=sl: E.scalar_tensor_tensor(
                    pfst[sl][:, 4:5], pfst[sl][:, 0:1], 1.0 / D, pfst[sl][:, 3:4],
                    ALU.mult, ALU.subtract), reads=[bst], writes=[bst])
                Q.do("scalar", lambda E, sl=sl: E.activation(
                    pfst[sl][:, 5:6], pfst[sl][:, 4:5], AF.Ln, bias=cp("eps_ln")),
                    reads=[bst, B("cpool")], writes=[bst])
                Q.do("scalar", lambda E, sl=sl: E.activation(
                    pfst[sl][:, 6:7], pfst[sl][:, 5:6], AF.Exp, scale=-0.5),
                    reads=[bst], writes=[bst])
                Q.do("vector", lambda E, sl=sl: E.tensor_tensor(
                    pfst[sl][:, 7:8], pfst[sl][:, 2:3], pfst[sl][:, 6:7], ALU.mult),
                    reads=[bst], writes=[bst])
                Q.do("scalar", lambda E, sl=sl: E.activation(
                    pfb[sl][:], pfh[sl][:], AF.Identity, scale=pfst[sl][:, 6:7],
                    bias=pfst[sl][:, 7:8]), reads=[bh, bst], writes=[bb])
                Q.do("gpsimd", lambda E, cg=cg, sl=sl: E.dma_start(
                    hb_dram[cg * 128:(cg + 1) * 128, :], pfb[sl][:]),
                    reads=[bb], writes=[B("hb_dram")], dma="dG")

            def fa_front(c, Qpre, Q):
                sl = c % 2
                P = lambda nm: B(f"{nm}{sl}")
                Qpre.do("sync", lambda E, c=c, sl=sl: E.dma_start(
                    xr_in[sl][:], base_dram[c * 128:(c + 1) * 128, :]),
                    reads=[B(base_buf)], writes=[P("fa_xrin")], dma="dS")
                if c == 0:
                    for t in range(4, 8):
                        Qpre.do("vector", lambda E, t=t, sl=sl: E.memset(
                            hsh[sl][:, (t - 4) * 128:(t - 4) * 128 + 1], 0.0),
                            writes=[P("fa_hsh")])
                        Qpre.do("vector", lambda E, t=t, sl=sl: E.tensor_copy(
                            hsh[sl][:, (t - 4) * 128 + 1:(t - 3) * 128],
                            srcT[:, t * N: t * N + 127]),
                            reads=[B("srcT")], writes=[P("fa_hsh")])

                def qkvgrp(E, c=c, sl=sl):
                    last = None
                    for kc in range(8):
                        if kc < 4:
                            lhs = srcT[:, kc * N + c * 128: kc * N + (c + 1) * 128]
                        elif c == 0:
                            lhs = hsh[sl][:, (kc - 4) * 128:(kc - 3) * 128]
                        else:
                            lhs = srcT[:, kc * N + c * 128 - 1: kc * N + (c + 1) * 128 - 1]
                        E.matmul(PB[0][:], lhs, wqkv_bf[:, kc * 1536: kc * 1536 + 512],
                                 start=(kc == 0), stop=(kc == 7))
                        E.matmul(PB[1][:], lhs, wqkv_bf[:, kc * 1536 + 512: kc * 1536 + 1024],
                                 start=(kc == 0), stop=(kc == 7))
                        last = E.matmul(PB[2][:], lhs,
                                        wqkv_bf[:, kc * 1536 + 1024: kc * 1536 + 1536],
                                        start=(kc == 0), stop=(kc == 7))
                    return last
                Qpre.do("tensor", qkvgrp, reads=[B("srcT"), P("fa_hsh"), B("fa_wqkv")],
                     writes=[B("pb0"), B("pb1"), B("pb2")])
                Qpre.do("scalar", lambda E, c=c: E.copy(
                    vaugF4[:, c, :, 0:64],
                    PB[2][:].rearrange("p (h e) -> p h e", h=FH)),
                    reads=[B("pb2")], writes=[B(f"fa_vaug_{c}")])
                Qpre.do("scalar", lambda E, sl=sl: E.copy(k_f32[sl][:], PB[1][:]),
                     reads=[B("pb1")], writes=[P("fa_kf32")])
                Qpre.do("scalar", lambda E, sl=sl: E.copy(
                    q_f32[sl][:], PB[0][:]), reads=[B("pb0")], writes=[P("fa_qf32")])

                Q.do("scalar", lambda E, sl=sl: E.activation(
                    a_q[sl][:], q_f32[sl][:], AF.Square),
                    reads=[P("fa_qf32")], writes=[P("fa_aq")])
                Q.do("vector", lambda E, sl=sl: E.reduce_sum(
                    ssq[sl][:, 0:FH].rearrange("p (h e) -> p h e", e=1),
                    a_q[sl][:].rearrange("p (h e) -> p h e", h=FH), axis=AX.X),
                    reads=[P("fa_aq")], writes=[P("fa_ssq")])
                Q.do("gpsimd", lambda E, sl=sl: E.tensor_tensor(
                    a_k[sl][:], k_f32[sl][:], k_f32[sl][:], ALU.mult),
                    reads=[P("fa_kf32")], writes=[P("fa_ak")])
                Q.do("vector", lambda E, sl=sl: E.reduce_sum(
                    ssq[sl][:, FH:2 * FH].rearrange("p (h e) -> p h e", e=1),
                    a_k[sl][:].rearrange("p (h e) -> p h e", h=FH), axis=AX.X),
                    reads=[P("fa_ak")], writes=[P("fa_ssq")])
                Q.do("vector", lambda E, sl=sl: E.tensor_scalar_max(
                    ssq[sl][:], ssq[sl][:], 1e-24),
                    reads=[P("fa_ssq")], writes=[P("fa_ssq")])
                Q.do("scalar", lambda E, sl=sl: E.activation(
                    inv[sl][:], ssq[sl][:], AF.Ln),
                    reads=[P("fa_ssq")], writes=[P("fa_inv")])
                Q.do("scalar", lambda E, sl=sl: E.activation(
                    inv[sl][:], inv[sl][:], AF.Exp, scale=-0.5),
                    reads=[P("fa_inv")], writes=[P("fa_inv")])
                Q.do("gpsimd", lambda E, sl=sl: E.tensor_tensor(
                    u_q[sl][:].rearrange("p (h e) -> p h e", h=FH),
                    q_f32[sl][:].rearrange("p (h e) -> p h e", h=FH),
                    inv[sl][:, 0:FH].unsqueeze(2).broadcast_to([128, FH, FD]),
                    ALU.mult), reads=[P("fa_qf32"), P("fa_inv")],
                    writes=[P("fa_qf32")])
                Q.do("gpsimd", lambda E, c=c, sl=sl: E.tensor_tensor(
                    a_q[sl][:].rearrange("p (h e) -> p h e", h=FH),
                    u_q[sl][:].rearrange("p (h e) -> p h e", h=FH),
                    tq1c[:, c, :].unsqueeze(1).broadcast_to([128, FH, FD]),
                    ALU.mult), reads=[P("fa_qf32"), B("fa_tq1")], writes=[P("fa_aq")])
                uq3 = u_q[sl][:].rearrange("p (h e) -> p h e", h=FH)
                bq3 = b_q[sl][:].rearrange("p (h e) -> p h e", h=FH)
                Q.do("gpsimd", lambda E, c=c, uq3=uq3, bq3=bq3: E.tensor_tensor(
                    bq3[:, :, 0:32], uq3[:, :, 32:64],
                    tq2c[:, c, 0:32].unsqueeze(1).broadcast_to([128, FH, 32]),
                    ALU.mult), reads=[P("fa_qf32"), B("fa_tq2")], writes=[P("fa_bq")])
                Q.do("gpsimd", lambda E, c=c, uq3=uq3, bq3=bq3: E.tensor_tensor(
                    bq3[:, :, 32:64], uq3[:, :, 0:32],
                    tq2c[:, c, 32:64].unsqueeze(1).broadcast_to([128, FH, 32]),
                    ALU.mult), reads=[P("fa_qf32"), B("fa_tq2")], writes=[P("fa_bq")])
                Q.do("gpsimd", lambda E, sl=sl: E.tensor_add(
                    qrot[sl][:], a_q[sl][:], b_q[sl][:]),
                    reads=[P("fa_aq"), P("fa_bq")], writes=[P("fa_qrot")])
                Q.do("gpsimd", lambda E, sl=sl: E.tensor_tensor(
                    u_k[sl][:].rearrange("p (h e) -> p h e", h=FH),
                    k_f32[sl][:].rearrange("p (h e) -> p h e", h=FH),
                    inv[sl][:, FH:2 * FH].unsqueeze(2).broadcast_to([128, FH, FD]),
                    ALU.mult), reads=[P("fa_kf32"), P("fa_inv")],
                    writes=[P("fa_kf32")])
                Q.do("gpsimd", lambda E, c=c, sl=sl: E.tensor_tensor(
                    a_k[sl][:].rearrange("p (h e) -> p h e", h=FH),
                    u_k[sl][:].rearrange("p (h e) -> p h e", h=FH),
                    tk1c[:, c, :].unsqueeze(1).broadcast_to([128, FH, FD]),
                    ALU.mult), reads=[P("fa_kf32"), B("fa_tk1")], writes=[P("fa_ak")])
                uk3 = u_k[sl][:].rearrange("p (h e) -> p h e", h=FH)
                bk3 = b_k[sl][:].rearrange("p (h e) -> p h e", h=FH)
                Q.do("gpsimd", lambda E, c=c, uk3=uk3, bk3=bk3: E.tensor_tensor(
                    bk3[:, :, 0:32], uk3[:, :, 32:64],
                    tk2c[:, c, 0:32].unsqueeze(1).broadcast_to([128, FH, 32]),
                    ALU.mult), reads=[P("fa_kf32"), B("fa_tk2")], writes=[P("fa_bk")])
                Q.do("gpsimd", lambda E, c=c, uk3=uk3, bk3=bk3: E.tensor_tensor(
                    bk3[:, :, 32:64], uk3[:, :, 0:32],
                    tk2c[:, c, 32:64].unsqueeze(1).broadcast_to([128, FH, 32]),
                    ALU.mult), reads=[P("fa_kf32"), B("fa_tk2")], writes=[P("fa_bk")])
                Q.do("gpsimd", lambda E, sl=sl: E.tensor_add(
                    krot[sl][:], a_k[sl][:], b_k[sl][:]),
                    reads=[P("fa_ak"), P("fa_bk")], writes=[P("fa_krot")])

            def fa_mid(c, Q):
                sl = c % 2
                P = lambda nm: B(f"{nm}{sl}")
                for p in range(4):
                    Q.do("tensor", lambda E, p=p, sl=sl: E.transpose(
                        PBbf[2][:, p * 128:(p + 1) * 128],
                        qrot[sl][:, p * 128:(p + 1) * 128], ident[:]),
                        reads=[P("fa_qrot"), B("c_ident")], writes=[B("pb2")])
                Q.do("scalar", lambda E, sl=sl: E.copy(
                    qTz[sl][0:64, :].rearrange("p (t c) -> p t c", t=8)[:, 0::2, :],
                    PBbf[2][0:64, 0:512].rearrange("p (t c) -> p t c", t=4)),
                    reads=[B("pb2")], writes=[P("fa_qTz")])
                Q.do("scalar", lambda E, sl=sl: E.copy(
                    qTz[sl][64:128, :].rearrange("p (t c) -> p t c", t=8)[:, 1::2, :],
                    PBbf[2][64:128, 0:512].rearrange("p (t c) -> p t c", t=4)),
                    reads=[B("pb2")], writes=[P("fa_qTz")])
                for p in range(4):
                    Q.do("tensor", lambda E, p=p, sl=sl: E.transpose(
                        PBbf[3][:, p * 128:(p + 1) * 128],
                        krot[sl][:, p * 128:(p + 1) * 128], ident[:]),
                        reads=[P("fa_krot"), B("c_ident")], writes=[B("pb3")])
                Q.do("scalar", lambda E, c=c: E.copy(
                    kTf[:].rearrange("p (t c) -> p t c", c=N)[:, :, c * 128:(c + 1) * 128],
                    PBbf[3][:, 0:512].rearrange("p (t c) -> p t c", t=4)),
                    reads=[B("pb3")], writes=[B(f"fa_kTf_{c}")])

            def fa_back(c, dq):
                sl = c % 2
                P = lambda nm: B(f"{nm}{sl}")
                kc0 = max(0, c - 4)
                nk = (c - kc0 + 1) * 128
                n1 = min(nk, 512)
                ntile = nk // 128

                def head_front(h):
                    hp = h % 4
                    sA, sB = (4, 5) if h % 2 == 0 else (0, 1)
                    tp = h // 2

                    def scgrp(E, h=h, tp=tp, kc0=kc0, nk=nk, n1=n1,
                              sA=sA, sB=sB, sl=sl):
                        last = E.matmul(PB[sA][:, 0:n1], qTz[sl][:, h * 128:(h + 1) * 128],
                                        kTf[:, tp * N + kc0 * 128: tp * N + kc0 * 128 + n1],
                                        start=True, stop=True)
                        if nk > 512:
                            last = E.matmul(PB[sB][:, 0:128], qTz[sl][:, h * 128:(h + 1) * 128],
                                            kTf[:, tp * N + kc0 * 128 + 512: tp * N + kc0 * 128 + nk],
                                            start=True, stop=True)
                        return last
                    S.do("tensor", scgrp,
                         reads=[P("fa_qTz")] + [B(f"fa_kTf_{j}")
                                                for j in range(kc0, c + 1)],
                         writes=[B(f"pb{sA}")] + ([B(f"pb{sB}")] if nk > 512 else []))
                    S.do("vector", lambda E, h=h, n1=n1, sA=sA: E.reduce_max(
                        nmx[:, h:h + 1], PB[sA][:, 0:n1], axis=AX.X, negate=True),
                        reads=[B(f"pb{sA}")], writes=[B("fa_nmx")])
                    if nk > 512:
                        S.do("vector", lambda E, h=h, sB=sB: E.reduce_max(
                            nmx[:, FH + h:FH + h + 1], PB[sB][:, 0:128], axis=AX.X,
                            negate=True),
                            reads=[B(f"pb{sB}")], writes=[B("fa_nmx")])
                        S.do("vector", lambda E, h=h: E.tensor_tensor(
                            nmx[:, h:h + 1], nmx[:, h:h + 1],
                            nmx[:, FH + h:FH + h + 1], ALU.min),
                            reads=[B("fa_nmx")], writes=[B("fa_nmx")])
                    S.do("scalar", lambda E, h=h, n1=n1, sA=sA, hp=hp: E.activation(
                        pbuf[hp][:, 0:n1], PB[sA][:, 0:n1], AF.Exp,
                        bias=nmx[:, h:h + 1]),
                        reads=[B(f"pb{sA}"), B("fa_nmx")], writes=[B(f"fa_pbuf{hp}")])
                    if nk > 512:
                        S.do("scalar", lambda E, h=h, sB=sB, hp=hp: E.activation(
                            pbuf[hp][:, 512:640], PB[sB][:, 0:128], AF.Exp,
                            bias=nmx[:, h:h + 1]),
                            reads=[B(f"pb{sB}"), B("fa_nmx")], writes=[B(f"fa_pbuf{hp}")])

                def head_back(h):
                    hp = h % 4
                    pb = 2 + (h % 2)
                    for j in range(ntile):
                        S.do("tensor", lambda E, j=j, pb=pb, hp=hp: E.transpose(
                            PBbf[pb][:, j * 128:(j + 1) * 128],
                            pbuf[hp][:, j * 128:(j + 1) * 128], ident[:]),
                            reads=[B(f"fa_pbuf{hp}"), B("c_ident")], writes=[B(f"pb{pb}")])
                    msk = m_cfull if c >= 4 else m_ctail
                    moff = 0 if c >= 4 else 640 - ntile * 128
                    S.do("vector", lambda E, pb=pb, h=h, msk=msk, moff=moff,
                         ntile=ntile: E.tensor_tensor(
                            pT[h % 2][:, 0:ntile * 128], PBbf[pb][:, 0:ntile * 128],
                            msk[:, moff:moff + ntile * 128], ALU.mult),
                            reads=[B(f"pb{pb}"), B("c_m_cfull"), B("c_m_ctail")],
                            writes=[B(f"fa_pT{h % 2}")])
                    avb = 6 + h // 4

                    def avgrp(E, h=h, kc0=kc0, ntile=ntile, avb=avb, hp=hp):
                        last = None
                        for j in range(ntile):
                            last = E.matmul(
                                PB[avb][:, (h % 4) * 65:(h % 4) * 65 + 65],
                                pT[h % 2][:, j * 128:(j + 1) * 128],
                                vaugF[:, (kc0 + j) * 520 + h * 65:(kc0 + j) * 520 + h * 65 + 65],
                                start=(j == 0), stop=(j == ntile - 1))
                        return last
                    S.do("tensor", avgrp,
                         reads=[B(f"fa_pT{h % 2}"), B("fa_vaug_ones")]
                               + [B(f"fa_vaug_{j}") for j in range(kc0, c + 1)],
                         writes=[B(f"pb{avb}")])

                def drain(n):
                    for _ in range(n):
                        if dq:
                            a, k = dq.pop(0)
                            S.do(*a, **k)
                for step in range(FH + 3):
                    if step < FH:
                        head_front(step)
                    if step >= 3:
                        head_back(step - 3)
                    if step == 2:
                        drain(4)
                    elif step > 2:
                        drain((len(dq) + FH + 1 - step) // max(1, FH + 2 - step))
                drain(len(dq))

                for g, avb in ((0, 6), (1, 7)):
                    pbg3 = PB[avb][:, 0:260].rearrange("p (i e) -> p i e", i=4)
                    S.do("vector", lambda E, g=g, pbg3=pbg3: E.tensor_scalar(
                        dent[:, g * 4:(g + 1) * 4].rearrange("p (i e) -> p i e", e=1),
                        pbg3[:, :, 64:65], 1e-30, None, ALU.add),
                        reads=[B(f"pb{avb}")], writes=[B("fa_dent")])
                    S.do("vector", lambda E, g=g: E.reciprocal(
                        drec[:, g * 4:(g + 1) * 4], dent[:, g * 4:(g + 1) * 4]),
                        reads=[B("fa_dent")], writes=[B("fa_drec")])
                    S.do("vector", lambda E, g=g, pbg3=pbg3, sl=sl: E.scalar_tensor_tensor(
                        o_pair[sl][:, g * 256:(g + 1) * 256].rearrange(
                            "p (i e) -> p i e", i=4),
                        pbg3[:, :, 0:64], 1.0,
                        drec[:, g * 4:(g + 1) * 4].unsqueeze(2).broadcast_to(
                            [128, 4, 64]),
                        ALU.mult, ALU.mult),
                        reads=[B(f"pb{avb}"), B("fa_drec")], writes=[P("fa_opair")])
                for p in range(4):
                    S.do("tensor", lambda E, p=p, sl=sl: E.transpose(
                        PBbf[2][:, p * 128:(p + 1) * 128],
                        o_pair[sl][:, p * 128:(p + 1) * 128], ident[:]),
                        reads=[P("fa_opair"), B("c_ident")], writes=[B("pb2")])
                S.do("scalar", lambda E, sl=sl: E.copy(
                    oT[sl][:], PBbf[2][:, 0:512]),
                    reads=[B("pb2")], writes=[P("fa_oT")])

                def wogrp(E, sl=sl):
                    last = None
                    for ic in range(4):
                        lhs = oT[sl][:, ic * 128:(ic + 1) * 128]
                        E.matmul(PB[6][:], lhs, wof_bf[:, ic * 1024: ic * 1024 + 512],
                                 start=(ic == 0), stop=(ic == 3))
                        last = E.matmul(PB[7][:], lhs,
                                        wof_bf[:, ic * 1024 + 512: ic * 1024 + 1024],
                                        start=(ic == 0), stop=(ic == 3))
                    return last
                S.do("tensor", wogrp, reads=[P("fa_oT"), B("fa_wof")],
                     writes=[B("pb6"), B("pb7")])
                S.do("vector", lambda E, sl=sl: E.tensor_tensor(
                    xf_out[sl][:, 0:512], PB[6][:], xr_in[sl][:, 0:512], ALU.add),
                    reads=[B("pb6")], writes=[P("fa_xrin")])
                S.do("vector", lambda E, sl=sl: E.tensor_tensor(
                    xf_out[sl][:, 512:1024], PB[7][:], xr_in[sl][:, 512:1024], ALU.add),
                    reads=[B("pb7")], writes=[P("fa_xrin")])
                S.do("gpsimd", lambda E, c=c, sl=sl: E.dma_start(
                    xf_dram[c * 128:(c + 1) * 128, :], xf_out[sl][:]),
                    reads=[P("fa_xrin")], writes=[B("xf_dram")], dma="dG")

            d0p, d0 = Defer(), Defer()
            fa_front(0, d0p, d0)
            fa_mid(0, d0)
            for a, k in d0p.items + d0.items:
                S.do(*a, **k)
            for c in range(NCH):
                dpre, dq, dmid = Defer(), Defer(), Defer()
                if c + 1 < NCH:
                    fa_front(c + 1, dpre, dq)
                    fa_mid(c + 1, dmid)
                pre = dpre.items
                if pre:
                    a, k = pre[0]
                    S.do(*a, **k)
                    pre = pre[1:]
                dpf = Defer()
                if c >= 8:
                    pf_front(c - 8, dpf)
                fa_back(c, pre + dq.items + dpf.items)
                for a, k in dmid.items:
                    S.do(*a, **k)

        def stageFm(sc, l):
            ssb = lambda nm, shp, dt: sc.enter_context(
                nc.sbuf_tensor(f"l{l}" + nm, shp, dt))
            final = (l == 1)
            NHC = 22
            h2nT = [ssb(f"fm_h2nT{i}", [128, 8 * 1024], BF16) for i in range(2)]
            gated = ssb("fm_gated", [128, NHC * 1024], BF16)
            w2_bf = ssb("fm_w2", [128, NHC * 1024], BF16)
            wsl = [ssb(f"fm_wsl{i}", [128, 8 * 128], BF16) for i in range(4)]
            b2row = ssb("fm_b2row", [1, D], BF16)
            lngc = ssb("fm_lngc", [128, 8], F32)
            lnbc = ssb("fm_lnbc", [128, 8], BF16)
            b1c = ssb("fm_b1c", [128, 2 * NHC], F32)
            beff = ssb("fm_beff", [128, 2 * NHC], F32)
            h2 = [ssb(f"fm_h2_{i}", [128, D], F32) for i in range(3)]
            sqs = ssb("fm_sqs", [128, D], BF16)
            st = [ssb(f"fm_st{i}", [128, 8], F32) for i in range(4)]
            h2b = [ssb(f"fm_h2b{i}", [128, D], BF16) for i in range(2)]
            gsb = [ssb(f"fm_gsb{i}", [128, 1024], BF16) for i in range(2)]
            xfm = [ssb(f"fm_xfm{i}", [128, D], F32) for i in range(2)]
            xam = [ssb(f"fm_xam{i}", [128, D], F32) for i in range(2)]
            outb = xfm

            S.do("gpsimd", lambda E: E.dma_start(
                w2_bf[:, 0:21 * 1024].rearrange("p (t c) -> p t c", t=21),
                h_in[f"w2_{l}"][0:2688, :].rearrange("(t p) c -> p t c", p=128)),
                writes=[B("fm_w2")], dma="dW")
            S.do("gpsimd", lambda E: E.dma_start(
                w2_bf[0:42, 21 * 1024:22 * 1024],
                h_in[f"w2_{l}"][2688:2730, :]),
                writes=[B("fm_w2")], dma="dW")
            S.do("gpsimd", lambda E: E.dma_start(b2row[:], h_in[f"b2_{l}"]),
                 writes=[B("fm_b2row")], dma="dW")
            S.do("sync", lambda E: E.dma_start(lngc[:], h_in[f"lng{l}"]),
                 writes=[B("fm_lngc")], dma="dS")
            S.do("gpsimd", lambda E: E.dma_start(lnbc[:], h_in[f"lnb{l}"]),
                 writes=[B("fm_lnbc")], dma="dW")
            S.do("sync", lambda E: E.dma_start(b1c[:], h_in[f"b1c{l}"]),
                 writes=[B("fm_b1c")], dma="dS")

            def load_w1_slice(side, hc, sl):
                blk = (side * 22 + hc) * 128
                S.do("gpsimd", lambda E, sl=sl, blk=blk: E.dma_start(
                    wsl[sl][:], h_in[f"w1s{l}"][blk:blk + 128, :]),
                    writes=[B(f"fm_wsl{sl}")], dma="dW")

            def pre_front(cg):
                sl = cg % 3
                mg = cg // 8
                bh = B(f"fm_h2_{sl}")
                S.do("sync", lambda E, cg=cg, sl=sl: E.dma_start(
                    h2[sl][:, 0:512], xf_dram[cg * 128:(cg + 1) * 128, 0:512]),
                    reads=[B("xf_dram")], writes=[bh], dma="dS")
                if cg > 0:
                    S.do("sync", lambda E, cg=cg, sl=sl: E.dma_start(
                        h2[sl][:, 512:1024],
                        xf_dram[cg * 128 - 1:(cg + 1) * 128 - 1, 512:1024]),
                        reads=[B("xf_dram")], writes=[bh], dma="dS")
                else:
                    S.do("vector", lambda E, sl=sl: E.memset(
                        h2[sl][0:1, 512:1024], 0.0), writes=[bh])
                    S.do("sync", lambda E, sl=sl: E.dma_start(
                        h2[sl][1:128, 512:1024], xf_dram[0:127, 512:1024]),
                        reads=[B("xf_dram")], writes=[bh], dma="dS")
                bst = B(f"fm_st{sl}")
                S.do("scalar", lambda E, sl=sl: E.activation(
                    sqs[:], h2[sl][:], AF.Square, accum_out=st[sl][:, 0:1]),
                    reads=[bh], writes=[B("fm_sqs"), bst])
                S.do("vector", lambda E, sl=sl: E.reduce_sum(
                    st[sl][:, 1:2], h2[sl][:], axis=AX.X), reads=[bh], writes=[bst])
                S.do("vector", lambda E, sl=sl: E.tensor_scalar(
                    st[sl][:, 2:3], st[sl][:, 1:2], -1.0 / D, None, ALU.mult),
                    reads=[bst], writes=[bst])
                S.do("vector", lambda E, sl=sl: E.tensor_tensor(
                    st[sl][:, 3:4], st[sl][:, 2:3], st[sl][:, 2:3], ALU.mult),
                    reads=[bst], writes=[bst])
                S.do("vector", lambda E, sl=sl: E.scalar_tensor_tensor(
                    st[sl][:, 4:5], st[sl][:, 0:1], 1.0 / D, st[sl][:, 3:4],
                    ALU.mult, ALU.subtract), reads=[bst], writes=[bst])
                S.do("scalar", lambda E, sl=sl: E.activation(
                    st[sl][:, 5:6], st[sl][:, 4:5], AF.Ln, bias=cp("eps_ln")),
                    reads=[bst, B("cpool")], writes=[bst])
                S.do("scalar", lambda E, sl=sl: E.activation(
                    st[sl][:, 6:7], st[sl][:, 5:6], AF.Exp, scale=-0.5),
                    reads=[bst], writes=[bst])
                S.do("vector", lambda E, sl=sl: E.tensor_tensor(
                    st[sl][:, 7:8], st[sl][:, 2:3], st[sl][:, 6:7], ALU.mult),
                    reads=[bst], writes=[bst])
                hp = cg % 2
                bb = B(f"fm_h2b{hp}")
                S.do("scalar", lambda E, sl=sl, hp=hp: E.activation(
                    h2b[hp][:], h2[sl][:], AF.Identity, scale=st[sl][:, 6:7],
                    bias=st[sl][:, 7:8]), reads=[bh, bst], writes=[bb])

            def pre_back(cg):
                mg = cg // 8
                hp = cg % 2
                bb = B(f"fm_h2b{hp}")
                cc = cg % 8
                h2nT3 = h2nT[mg][:].rearrange("p (t n) -> p t n", t=8)
                for grp in range(2):
                    pb = 6 + grp
                    for j in range(4):
                        ic = grp * 4 + j
                        S.do("tensor", lambda E, ic=ic, j=j, pb=pb, hp=hp: E.transpose(
                            PBbf[pb][:, j * 128:(j + 1) * 128],
                            h2b[hp][:, ic * 128:(ic + 1) * 128],
                            ident[:]), reads=[bb, B("c_ident")], writes=[B(f"pb{pb}")])
                    S.do("vector", lambda E, grp=grp, pb=pb, cc=cc, h2nT3=h2nT3: E.tensor_tensor(
                        h2nT3[:, grp * 4:(grp + 1) * 4, cc * 128:(cc + 1) * 128],
                        PBbf[pb][:, 0:512].rearrange("p (t n) -> p t n", t=4),
                        lngc[:, grp * 4:(grp + 1) * 4].unsqueeze(2).broadcast_to(
                            [128, 4, 128]), ALU.mult),
                        reads=[B(f"pb{pb}"), B("fm_lngc")], writes=[B(f"fm_h2nT{mg}")])

            def w1_phase(mega):
                for hc in range(NHC):
                    cw = 42 if hc == 21 else 128
                    bank4 = (0, 1, 2, 3) if hc % 2 == 0 else (4, 5, 6, 7)
                    for side in (0, 1):
                        sl = (hc * 2 + side) % 4
                        load_w1_slice(side, hc, sl)
                        pA, pB = bank4[0:2] if side == 0 else bank4[2:4]

                        def w1grp(E, sl=sl, cw=cw, pA=pA, pB=pB, mega=mega):
                            last = None
                            for kc in range(8):
                                lhs = wsl[sl][:, kc * 128:kc * 128 + cw]
                                E.matmul(PB[pA][0:cw, :], lhs,
                                         h2nT[mega][:, kc * 1024: kc * 1024 + 512],
                                         start=(kc == 0), stop=(kc == 7))
                                last = E.matmul(PB[pB][0:cw, :], lhs,
                                                h2nT[mega][:, kc * 1024 + 512: kc * 1024 + 1024],
                                                start=(kc == 0), stop=(kc == 7))
                            return last
                        S.do("tensor", w1grp,
                             reads=[B(f"fm_wsl{sl}"), B(f"fm_h2nT{mega}")],
                             writes=[B(f"pb{pA}"), B(f"pb{pB}")])
                        if mega == 0:
                            bb_ = 7 if hc % 2 == 0 else 3

                            def bgrp(E, sl=sl, cw=cw, bb_=bb_):
                                last = None
                                for kc in range(8):
                                    last = E.matmul(
                                        PB[bb_][0:cw, 0:1],
                                        wsl[sl][:, kc * 128:kc * 128 + cw],
                                        lnbc[:, kc:kc + 1],
                                        start=(kc == 0), stop=(kc == 7))
                                return last
                            S.do("tensor", bgrp,
                                 reads=[B(f"fm_wsl{sl}"), B("fm_lnbc")],
                                 writes=[B(f"pb{bb_}")])
                            S.do("vector", lambda E, cw=cw, side=side, hc=hc, bb_=bb_: E.tensor_tensor(
                                beff[0:cw, side * NHC + hc: side * NHC + hc + 1],
                                PB[bb_][0:cw, 0:1],
                                b1c[0:cw, side * NHC + hc: side * NHC + hc + 1],
                                ALU.add),
                                reads=[B(f"pb{bb_}"), B("fm_b1c")], writes=[B("fm_beff")])
                    cw = 42 if hc == 21 else 128
                    hp = hc % 2
                    for half in range(2):
                        ga, gg = bank4[half], bank4[2 + half]
                        S.do("scalar", lambda E, hc=hc, cw=cw, gg=gg, half=half, hp=hp: E.activation(
                            gsb[hp][0:cw, half * 512:(half + 1) * 512],
                            PB[gg][0:cw, :], AF.Gelu,
                            bias=beff[0:cw, NHC + hc: NHC + hc + 1]),
                            reads=[B(f"pb{gg}"), B("fm_beff")],
                            writes=[B(f"fm_gsb{hp}")])
                        S.do("vector", lambda E, hc=hc, cw=cw, ga=ga, half=half, hp=hp: E.scalar_tensor_tensor(
                            gated[0:cw, hc * 1024 + half * 512: hc * 1024 + (half + 1) * 512],
                            PB[ga][0:cw, :],
                            beff[0:cw, hc:hc + 1],
                            gsb[hp][0:cw, half * 512:(half + 1) * 512],
                            ALU.add, ALU.mult),
                            reads=[B(f"pb{ga}"), B("fm_beff"), B(f"fm_gsb{hp}")],
                            writes=[B("fm_gated")])

            def w2_phase(mega):
                for cc in range(8):
                    cg = mega * 8 + cc
                    sl = cc % 2
                    S.do("sync", lambda E, cg=cg, sl=sl: E.dma_start(
                        xfm[sl][:], xf_dram[cg * 128:(cg + 1) * 128, :]),
                        reads=[B("xf_dram")], writes=[B(f"fm_xfm{sl}")], dma="dS")
                    if final:
                        S.do("sync", lambda E, cg=cg, sl=sl: E.dma_start(
                            xam[sl][:], xA_dram[cg * 128:(cg + 1) * 128, :]),
                            reads=[B("xA_done")], writes=[B(f"fm_xam{sl}")], dma="dS")

                    def w2grp(E, cc=cc):
                        for hc in range(NHC):
                            cw = 42 if hc == 21 else 128
                            lhs = gated[0:cw, hc * 1024 + cc * 128: hc * 1024 + (cc + 1) * 128]
                            E.matmul(PB[4][:], lhs,
                                     w2_bf[0:cw, hc * 1024: hc * 1024 + 512],
                                     start=(hc == 0), stop=False)
                            E.matmul(PB[5][:], lhs,
                                     w2_bf[0:cw, hc * 1024 + 512: hc * 1024 + 1024],
                                     start=(hc == 0), stop=False)
                        E.matmul(PB[4][:], ones_row[:], b2row[:, 0:512],
                                 start=False, stop=True)
                        return E.matmul(PB[5][:], ones_row[:], b2row[:, 512:1024],
                                        start=False, stop=True)
                    S.do("tensor", w2grp,
                         reads=[B("fm_gated"), B("fm_w2"), B("fm_b2row"),
                                B("ones_row")],
                         writes=[B("pb4"), B("pb5")])
                    bo_ = B(f"fm_xfm{sl}")
                    for half, pb in ((0, 4), (1, 5)):
                        S.do("vector", lambda E, sl=sl, half=half, pb=pb: E.tensor_tensor(
                            outb[sl][:, half * 512:(half + 1) * 512], PB[pb][:],
                            xfm[sl][:, half * 512:(half + 1) * 512], ALU.add),
                            reads=[B(f"pb{pb}")], writes=[bo_])
                    if final:
                        for half in range(2):
                            S.do("vector", lambda E, sl=sl, half=half: E.tensor_tensor(
                                outb[sl][:, half * 512:(half + 1) * 512],
                                outb[sl][:, half * 512:(half + 1) * 512],
                                xam[sl][:, half * 512:(half + 1) * 512], ALU.add),
                                reads=[bo_, B(f"fm_xam{sl}")], writes=[bo_])
                        S.do("gpsimd", lambda E, cg=cg, sl=sl: E.dma_start(
                            out_h[cg * 128:(cg + 1) * 128, :], outb[sl][:]),
                            reads=[bo_], dma="dG")
                    else:
                        S.do("gpsimd", lambda E, cg=cg, sl=sl: E.dma_start(
                            xf2_dram[cg * 128:(cg + 1) * 128, :], outb[sl][:]),
                            reads=[bo_], writes=[B("xf2_dram")], dma="dG")
                        srcT3 = srcT[:].rearrange("p (t n) -> p t n", t=8)
                        for grp, pb, eng in ((0, 0, "vector"), (1, 1, "scalar")):
                            for j in range(4):
                                ic = grp * 4 + j
                                S.do("tensor", lambda E, ic=ic, j=j, pb=pb, sl=sl: E.transpose(
                                    PB[pb][:, j * 128:(j + 1) * 128],
                                    outb[sl][:, ic * 128:(ic + 1) * 128],
                                    ident_f[:]), reads=[bo_, B("c_ident_f")],
                                    writes=[B(f"pb{pb}")])
                            S.do(eng, lambda E, cg=cg, grp=grp, pb=pb, srcT3=srcT3: COPY(
                                E, srcT3[:, grp * 4:(grp + 1) * 4, cg * 128:(cg + 1) * 128],
                                PB[pb][:, 0:512].rearrange("p (t n) -> p t n", t=4)),
                                reads=[B(f"pb{pb}")], writes=[B("srcT")])

            def load8(cc):
                hp = cc % 2
                S.do("sync", lambda E, cc=cc, hp=hp: E.dma_start(
                    h2b[hp][:], hb_dram[cc * 128:(cc + 1) * 128, :]),
                    reads=[B("hb_dram")], writes=[B(f"fm_h2b{hp}")], dma="dS")

            def prepass_group(cg0):
                pre_front(cg0)
                for cc in range(8):
                    if cc + 1 < 8:
                        pre_front(cg0 + cc + 1)
                    pre_back(cg0 + cc)

            load8(0)
            for cc in range(8):
                if cc + 1 < 8:
                    load8(cc + 1)
                pre_back(cc)
            w1_phase(0)
            prepass_group(8)
            w2_phase(0)
            w1_phase(1)
            w2_phase(1)

        prep()
        with ExitStack() as sc:
            s0f, s0b, s0xin = stage0(sc)
            stageA(sc, s0f, s0b, s0xin)
            S.barrier()
        bufs["xA_done"] = Buf("xA_done")
        bufs["xA_done"].w = {f"dG{i}": (S.cnt[f"dG{i}"], "dma")
                             for i in range(S.DMA_POOL)}
        bufs["base_done"] = Buf("base_done")
        bufs["base_done"].w = dict(bufs["xA_done"].w)
        with ExitStack() as sc:
            stageFa(sc, 0)
            S.barrier()
        with ExitStack() as sc:
            stageFm(sc, 0)
            S.barrier()
        with ExitStack() as sc:
            stageFa(sc, 1)
            S.barrier()
        with ExitStack() as sc:
            stageFm(sc, 1)
            S.barrier()

        if "xnT" in DBG:
            d = nc.dram_tensor("dbg_xnT", [128, 8 * N], F32,
                               kind="ExternalOutput").ap()
            dtmp = sb("dbg_tmp", [128, N], F32)
            for t in range(8):
                bt = B("dbg_tmp")
                S.do("vector", lambda E, t=t: E.tensor_copy(
                    dtmp[:], srcT[:, t * N:(t + 1) * N]),
                    reads=[B("srcT")], writes=[bt])
                S.do("sync", lambda E, t=t: E.dma_start(
                    d[:, t * N:(t + 1) * N], dtmp[:]), reads=[bt], dma="dS")

        for qn in ("dS", "dG", "dW", "dT"):
            S.dma_flush("sync", qn)

        block = ctx.enter_context(nc.Block())
        block.sync(lambda E: [f(E) for f in S.q["sync"]])
        block.vector(lambda E: [f(E) for f in S.q["vector"]])
        block.scalar(lambda E: [f(E) for f in S.q["scalar"]])
        block.tensor(lambda E: [f(E) for f in S.q["tensor"]])
        block.gpsimd(lambda E: [f(E) for f in S.q["gpsimd"]])

    return nc


def prep_inputs(L0_attn_x, L0_attn_omega, L0_attn_wq, L0_attn_wk, L0_attn_wv,
                L0_attn_wo, L0_attn_bo, L0_ffn_wqkv, L0_ffn_q_scale,
                L0_ffn_k_scale, L0_ffn_wo, L0_ffn_ln_g, L0_ffn_ln_b,
                L0_ffn_w1, L0_ffn_b1, L0_ffn_w2, L0_ffn_b2):
    f32 = lambda a: np.ascontiguousarray(np.asarray(a), dtype=np.float32)
    W = {}
    W["wqT"] = f32(np.asarray(L0_attn_wq).T)
    W["wkT"] = f32(np.asarray(L0_attn_wk).T)
    W["wvT"] = f32(np.asarray(L0_attn_wv).T)
    W["woT"] = f32(np.asarray(L0_attn_wo).T)
    omT = f32(np.asarray(L0_attn_omega).T)
    zz = np.zeros((64, 64), np.float32)
    W["om_top"] = f32(np.concatenate([omT, zz], axis=0))
    W["om_bot"] = f32(np.concatenate([zz, omT], axis=0))
    W["bo"] = f32(np.asarray(L0_attn_bo).reshape(1, D))
    for l in range(2):
        W[f"wqkv{l}"] = f32(np.asarray(L0_ffn_wqkv)[l])
        W[f"wof{l}"] = f32(np.asarray(L0_ffn_wo)[l])
        w1l = np.asarray(L0_ffn_w1)[l].astype(np.float32)
        w1p = np.zeros((D, 2, 22 * 128), np.float32)
        w1p[:, 0, 0:HID] = w1l[:, 0:HID]
        w1p[:, 1, 0:HID] = w1l[:, HID:2 * HID]
        W[f"w1s{l}"] = f32(w1p.reshape(8, 128, 2, 22, 128)
                           .transpose(2, 3, 1, 0, 4).reshape(44 * 128, D))
        W[f"b1_{l}"] = f32(np.asarray(L0_ffn_b1)[l].reshape(1, HID2))
        b1l = np.asarray(L0_ffn_b1)[l].astype(np.float32)
        bpad = np.zeros((2, 22 * 128), np.float32)
        bpad[0, 0:HID] = b1l[0:HID]
        bpad[1, 0:HID] = b1l[HID:2 * HID]
        W[f"b1c{l}"] = f32(np.concatenate(
            [bpad[0].reshape(22, 128).T, bpad[1].reshape(22, 128).T], axis=1))
        W[f"w2_{l}"] = f32(np.asarray(L0_ffn_w2)[l])
        W[f"b2_{l}"] = f32(np.asarray(L0_ffn_b2)[l].reshape(1, D))
        W[f"lng{l}"] = f32(np.asarray(L0_ffn_ln_g)[l].reshape(8, 128).T)
        W[f"lnb{l}"] = f32(np.asarray(L0_ffn_ln_b)[l].reshape(8, 128).T)
        W[f"qs{l}"] = f32(np.asarray(L0_ffn_q_scale)[l].reshape(1, FD))
        W[f"ks{l}"] = f32(np.asarray(L0_ffn_k_scale)[l].reshape(1, FD))
    W.update(host_consts())
    x = f32(L0_attn_x)
    return W, x


def kernel(**inputs):
    W, x = prep_inputs(**inputs)
    nc = build()
    in_maps = [dict(W, x=np.ascontiguousarray(x[b])) for b in range(8)]
    res = run_bass_kernel_spmd(nc, in_maps, core_ids=list(range(8)))
    out = np.stack([np.asarray(r["out"]) for r in res.results], axis=0)
    return out.astype(np.float32)
